# version 02
# baseline (speedup 1.0000x reference)
import numpy as np
import ml_dtypes
from contextlib import ExitStack

import concourse.bass as bass
import concourse.mybir as mybir
from concourse.bass_utils import run_bass_kernel_spmd

F32 = mybir.dt.float32
BF16 = mybir.dt.bfloat16
AF = mybir.ActivationFunctionType
OP = mybir.AluOpType
AX = mybir.AxisListType
BF = ml_dtypes.bfloat16
ds = bass.ds

B, S, D = 2, 4096, 2048
T = B * S
NC = 8
TOWN = T // NC
HQ, HKV, HD, ROT = 32, 8, 64, 32
FF = 5120
NQH = HQ // NC
QC = NQH * HD
NWE = FF * D


class Ctr:
    def __init__(self, nc, name):
        self.h = nc.alloc_semaphore(name)
        self.n = 0

    def inc(self, instr, k=1):
        instr.then_inc(self.h, k)
        self.n += k
        return self.n


def _rope_tables():
    inv = 1.0 / (10000.0 ** (np.arange(0, ROT, 2, dtype=np.float64) / ROT))
    fr = np.outer(np.arange(S, dtype=np.float64), inv)
    cos = np.cos(fr).astype(np.float32)
    sin = np.sin(fr).astype(np.float32)
    Ct = np.ones((128, S), np.float32)
    St = np.zeros((128, S), np.float32)
    for base in (0, 64):
        Ct[base + 0:base + 16] = cos.T
        Ct[base + 16:base + 32] = cos.T
        St[base + 0:base + 16] = -sin.T
        St[base + 16:base + 32] = sin.T
    return Ct, St


def _perm128():
    Pm = np.zeros((128, 128), np.float32)
    for p in range(128):
        g, r = divmod(p, 32)
        Pm[32 * g + ((r + 16) % 32), p] = 1.0
    return Pm


def build_program(debug_taps=(), upto="full", no_stats=False):
    nc = bass.Bass("TRN2", target_bir_lowering=False, debug=False,
                   detect_race_conditions=False)

    x_own = nc.dram_tensor("x_own", [TOWN, D], F32, kind="ExternalInput").ap()
    wq_d = nc.dram_tensor("wq", [D, QC], F32, kind="ExternalInput").ap()
    wk_d = nc.dram_tensor("wk", [D, HD], F32, kind="ExternalInput").ap()
    wv_d = nc.dram_tensor("wv", [D, HD], F32, kind="ExternalInput").ap()
    bq_d = nc.dram_tensor("bq", [QC, 1], F32, kind="ExternalInput").ap()
    bkv_d = nc.dram_tensor("bkv", [128, 1], F32, kind="ExternalInput").ap()
    bkd_d = nc.dram_tensor("bkd", [128, 1], F32, kind="ExternalInput").ap()
    wout_d = nc.dram_tensor("wout", [D, D], F32, kind="ExternalInput").ap()
    bout_d = nc.dram_tensor("bout", [1, D], F32, kind="ExternalInput").ap()
    wfcT_d = nc.dram_tensor("wfcT_sh", [D // NC, FF], F32, kind="ExternalInput").ap()
    wprT_d = nc.dram_tensor("wprT_sh", [FF // NC, D], F32, kind="ExternalInput").ap()
    out_d = nc.dram_tensor("out", [TOWN, D], F32, kind="ExternalOutput").ap()

    xnag_i = nc.dram_tensor("xnag_i", [D, TOWN], BF16)
    xnag_o = nc.dram_tensor("xnag_o", [NC * D, TOWN], BF16)
    stag_i = nc.dram_tensor("stag_i", [1, 2], F32)
    stag_o = nc.dram_tensor("stag_o", [1, 2], F32)
    wfag_i = nc.dram_tensor("wfag_i", [D // NC, FF], BF16)
    wfag_o = nc.dram_tensor("wfag_o", [D, FF], BF16)
    wpag_i = nc.dram_tensor("wpag_i", [FF // NC, D], BF16)
    wpag_o = nc.dram_tensor("wpag_o", [FF, D], BF16)
    atag_i = nc.dram_tensor("atag_i", [QC, T], BF16)
    atag_o = nc.dram_tensor("atag_o", [D, T], BF16)

    Cnp, Snp = _rope_tables()
    CT_d = nc.inline_tensor(Cnp, name="CTc")
    ST_d = nc.inline_tensor(Snp, name="STc")
    P_d = nc.inline_tensor(_perm128(), name="Pmc")
    tri_np = (np.arange(128)[None, :] >= np.arange(128)[:, None]).astype(BF)
    tri_d = nc.inline_tensor(tri_np, name="tric")
    ones_d = nc.inline_tensor(np.ones((128, 128), np.float32), name="onesc")
    id_d = nc.inline_tensor(np.eye(128, dtype=BF), name="idc")
    id64_np = np.zeros((128, 64), BF)
    id64_np[64:128] = np.eye(64, dtype=BF)
    id64_d = nc.inline_tensor(id64_np, name="id64c")

    taps = {}

    def tap(name, shape, dtype):
        if name in debug_taps:
            taps[name] = nc.dram_tensor("dbg_" + name, shape, dtype,
                                        kind="ExternalOutput")
        return taps.get(name)

    t_xnt = tap("xnt", [NC * D, TOWN], BF16)
    t_xno = tap("xno", [D, TOWN], BF16)
    t_tfc = tap("tfc", [D, FF], BF16)
    t_tpr = tap("tpr", [FF, D], BF16)
    t_qt = tap("qt", [QC, T], BF16)
    t_k2 = tap("k2", [128, T], BF16)
    t_vt = tap("vt", [128, 64 * 65], BF16)
    t_at = tap("at", [D, T], BF16)
    t_cb = tap("cb", [128, 8], F32)
    t_tfi = tap("tfi", [D // NC, FF], BF16)

    Cn = lambda name: Ctr(nc, name)
    sX, sSQ, sVA, sSD, sNM, sXN = (Cn("sX"), Cn("sSQ"), Cn("sVA"), Cn("sSD"),
                                   Cn("sNM"), Cn("sXN"))
    sTR, sXNT, sXC, sLDC, sWQ = (Cn("sTR"), Cn("sXNT"), Cn("sXC"), Cn("sLDC"),
                                 Cn("sWQ"))
    sTS, sSTv, sSTm, sSTe, sARd = (Cn("sTS"), Cn("sSTv"), Cn("sSTm"),
                                   Cn("sSTe"), Cn("sARd"))
    sCB, sSG, sTT, sTW = Cn("sCB"), Cn("sSG"), Cn("sTT"), Cn("sTW")
    sARc, sAGXc, sAGWc, sAGAc = (Cn("sARc"), Cn("sAGXc"), Cn("sAGWc"),
                                 Cn("sAGAc"))
    sRH, sCS, sQP, sQE, sQS = Cn("sRH"), Cn("sCS"), Cn("sQP"), Cn("sQE"), Cn("sQS")
    sT1, sT2, sRP, sTC, sTV, sVC = (Cn("sT1"), Cn("sT2"), Cn("sRP"), Cn("sTC"),
                                    Cn("sTV"), Cn("sVC"))
    sSC, sEXP, sTRI, sAV, sRC, sBC, sEPI, sSTG = (
        Cn("sSC"), Cn("sEXP"), Cn("sTRI"), Cn("sAV"), Cn("sRC"), Cn("sBC"),
        Cn("sEPI"), Cn("sSTG"))
    sFS, sFMM, sFE, sFH = Cn("sFS"), Cn("sFMM"), Cn("sFE"), Cn("sFH")
    sPC, sWC, sXR, sOP, sOE, sOS = (Cn("sPC"), Cn("sWC"), Cn("sXR"), Cn("sOP"),
                                    Cn("sOE"), Cn("sOS"))
    sDBG = Cn("sDBG")
    sTS2 = Cn("sTS2")
    sFNC = Cn("sFNC")

    def vf(v, ins):
        sFNC.inc(ins, 1)
        v.wait_ge(sFNC.h, sFNC.n)

    est = ExitStack()
    with est:
        xnT = est.enter_context(nc.sbuf_tensor("xnT", [128, 16 * TOWN], BF16))
        Pf = est.enter_context(nc.sbuf_tensor("Pf", [128, 128], F32))
        trib = est.enter_context(nc.sbuf_tensor("trib", [128, 128], BF16))
        onesf = est.enter_context(nc.sbuf_tensor("onesf", [128, 128], F32))
        idb = est.enter_context(nc.sbuf_tensor("idb", [128, 128], BF16))
        id64 = est.enter_context(nc.sbuf_tensor("id64", [128, 64], BF16))
        bq_sb = est.enter_context(nc.sbuf_tensor("bq_sb", [128, 2], F32))
        bkv_sb = est.enter_context(nc.sbuf_tensor("bkv_sb", [128, 1], F32))
        bkd_sb = est.enter_context(nc.sbuf_tensor("bkd_sb", [128, 1], F32))
        cb4 = est.enter_context(nc.sbuf_tensor("cb4", [128, 4], F32))
        stat4 = est.enter_context(nc.sbuf_tensor("stat4", [128, 4], F32))
        st2 = est.enter_context(nc.sbuf_tensor("st2", [1, 4], F32))
        sln = est.enter_context(nc.sbuf_tensor("sln", [128, 16], F32))
        qstage = [est.enter_context(nc.sbuf_tensor(f"qstage{i}", [128, 512], F32))
                  for i in range(2)]
        tmp1 = est.enter_context(nc.sbuf_tensor("tmp1", [128, 512], F32))
        vstage = est.enter_context(nc.sbuf_tensor("vstage", [128, 512], BF16))
        pstage = [est.enter_context(nc.sbuf_tensor(f"pstage{i}", [128, 2, 512], BF16))
                  for i in range(2)]
        stg = [est.enter_context(nc.sbuf_tensor(f"stg{i}", [64, 512], BF16))
               for i in range(4)]
        rs_sb = est.enter_context(nc.sbuf_tensor("rs_sb", [128, 2, 512], F32))
        rb_sb = est.enter_context(nc.sbuf_tensor("rb_sb", [128, 2, 512], F32))

        block = est.enter_context(nc.Block())

        @block.gpsimd
        def _(g):
            for dst, src in ((Pf[:, :], P_d.ap()), (trib[:, :], tri_d.ap()),
                             (onesf[:, :], ones_d.ap()), (idb[:, :], id_d.ap()),
                             (id64[:, :], id64_d.ap()),
                             (bq_sb[:, 0:1], bq_d[0:128, :]),
                             (bq_sb[:, 1:2], bq_d[128:256, :]),
                             (bkv_sb[:, :], bkv_d[:, :]),
                             (bkd_sb[:, :], bkd_d[:, :])):
                sLDC.inc(g.dma_start(out=dst, in_=src), 16)

        qkv_scope = ExitStack()
        with qkv_scope:
            QT = [qkv_scope.enter_context(nc.sbuf_tensor(f"QT{i}", [128, T], BF16))
                  for i in range(2)]
            K2 = qkv_scope.enter_context(nc.sbuf_tensor("K2", [128, T], BF16))
            Vt = qkv_scope.enter_context(nc.sbuf_tensor("Vt", [128, 64, 65], BF16))

            r1 = ExitStack()
            with r1:
                x_db = [r1.enter_context(nc.sbuf_tensor(f"x_db{i}", [128, D], F32))
                        for i in range(2)]
                xn_db = [r1.enter_context(nc.sbuf_tensor(f"xn_db{i}", [128, D], BF16))
                         for i in range(2)]
                sqdump = r1.enter_context(nc.sbuf_tensor("sqdump", [128, D], BF16))
                tsrc = [r1.enter_context(nc.sbuf_tensor(f"tsrc{i}", [128, 512], F32))
                        for i in range(2)]
                t1buf = r1.enter_context(nc.sbuf_tensor("t1buf", [128, 512], F32))
                t2buf = r1.enter_context(nc.sbuf_tensor("t2buf", [128, 512], F32))
                tstg = [r1.enter_context(nc.sbuf_tensor(f"tstg{i}", [128, 512], BF16))
                        for i in range(2)]
                csb = [r1.enter_context(nc.sbuf_tensor(f"csb{i}", [128, 2, 512], F32))
                       for i in range(2)]
                rhs_db = [r1.enter_context(
                    nc.sbuf_tensor(f"rhs{i}", [128, 16, 512], BF16)) for i in range(2)]
                wq_sb = r1.enter_context(nc.sbuf_tensor("wq_sb", [128, 16 * QC], BF16))
                wkv_sb = r1.enter_context(nc.sbuf_tensor("wkv_sb", [128, 16 * 128], BF16))

                pA = ExitStack()
                ptA = [pA.enter_context(nc.psum_tensor(f"ptA{i}", [128, 4, 128], BF16))
                       for i in range(2)]
                pmisc = pA.enter_context(nc.psum_tensor("pmisc", [128, 512], F32))

                @block.sync
                def _(sy):
                    for ti in range(8):
                        if ti >= 2:
                            sy.wait_ge(sXN.h, ti - 1)
                        sX.inc(sy.dma_start(
                            out=x_db[ti % 2][:, :],
                            in_=x_own[128 * ti:128 * (ti + 1), :]), 16)

                @block.scalar
                def _(sc):
                    for ti in range(9):
                        if ti < 8:
                            sc.wait_ge(sX.h, 16 * (ti + 1))
                            c = 8 * (ti % 2)
                            sSQ.inc(sc.activation(
                                out=sqdump[:, :], in_=x_db[ti % 2][:, :],
                                func=AF.Square, accum_out=sln[:, c + 4:c + 5]), 1)
                        if ti >= 1:
                            tj = ti - 1
                            c = 8 * (tj % 2)
                            sc.wait_ge(sVA.h, tj + 1)
                            sSD.inc(sc.activation(
                                out=sln[:, c + 6:c + 7], in_=sln[:, c + 5:c + 6],
                                func=AF.Sqrt), 1)
                            sc.wait_ge(sNM.h, tj + 1)
                            if tj >= 2:
                                sc.wait_ge(sTR.h, 4 * tj - 4)
                            sXN.inc(sc.activation(
                                out=xn_db[tj % 2][:, :], in_=x_db[tj % 2][:, :],
                                func=AF.Identity, bias=sln[:, c + 7:c + 8],
                                scale=sln[:, c + 6:c + 7]), 1)

                @block.vector
                def _(v):
                    inv_d = 1.0 / D
                    for ti in range(10):
                        if ti < 8:
                            c = 8 * (ti % 2)
                            v.wait_ge(sX.h, 16 * (ti + 1))
                            v.wait_ge(sSQ.h, ti + 1)
                            if ti >= 2:
                                v.wait_ge(sXN.h, ti - 1)
                            vf(v, v.reduce_sum(out=sln[:, c + 0:c + 1],
                                                in_=x_db[ti % 2][:, :], axis=AX.X))
                            vf(v, v.tensor_scalar(out=sln[:, c + 1:c + 2],
                                                  in0=sln[:, c + 0:c + 1],
                                                  scalar1=inv_d, scalar2=None,
                                                  op0=OP.mult))
                            vf(v, v.tensor_tensor(out=sln[:, c + 2:c + 3],
                                                  in0=sln[:, c + 1:c + 2],
                                                  in1=sln[:, c + 1:c + 2],
                                                  op=OP.mult))
                            vf(v, v.tensor_scalar(out=sln[:, c + 3:c + 4],
                                                  in0=sln[:, c + 4:c + 5],
                                                  scalar1=inv_d, scalar2=None,
                                                  op0=OP.mult))
                            vf(v, v.tensor_tensor(out=sln[:, c + 3:c + 4],
                                                  in0=sln[:, c + 3:c + 4],
                                                  in1=sln[:, c + 2:c + 3],
                                                  op=OP.subtract))
                            vf(v, v.tensor_scalar(out=sln[:, c + 3:c + 4],
                                                  in0=sln[:, c + 3:c + 4],
                                                  scalar1=1e-5, scalar2=None,
                                                  op0=OP.add))
                            sVA.inc(v.reciprocal(out=sln[:, c + 5:c + 6],
                                                 in_=sln[:, c + 3:c + 4]), 1)
                        if ti >= 1 and ti <= 8:
                            tj = ti - 1
                            c = 8 * (tj % 2)
                            v.wait_ge(sSD.h, tj + 1)
                            sNM.inc(v.tensor_scalar(
                                out=sln[:, c + 7:c + 8], in0=sln[:, c + 1:c + 2],
                                scalar1=sln[:, c + 6:c + 7], scalar2=-1.0,
                                op0=OP.mult, op1=OP.mult), 1)
                        if ti >= 2:
                            te2 = ti - 2
                            for gj in range(4):
                                G = 4 * te2 + gj
                                v.wait_ge(sTR.h, G + 1)
                                dst = xnT.ap().rearrange(
                                    "p (dj t) -> p dj t", dj=16)[
                                    :, 4 * gj:4 * gj + 4,
                                    128 * te2:128 * (te2 + 1)]
                                sXNT.inc(v.tensor_copy(out=dst,
                                                       in_=ptA[G % 2][:, :, :]), 1)
                    v.memset(Vt[:, :, 64:65], 1.0)

                @block.tensor
                def _(t):
                    t.wait_ge(sLDC.h, sLDC.n)
                    for ti in range(8):
                        t.wait_ge(sXN.h, ti + 1)
                        for gj in range(4):
                            G = 4 * ti + gj
                            if G >= 2:
                                t.wait_ge(sXNT.h, G - 1)
                            for i in range(4):
                                dj = 4 * gj + i
                                ins = t.transpose(
                                    ptA[G % 2][:, i, :],
                                    xn_db[ti % 2][:, 128 * dj:128 * (dj + 1)],
                                    idb[:, :])
                            sTR.inc(ins, 1)

                @block.sync
                def _(sy):
                    sy.wait_ge(sXNT.h, 32)
                    for dj in range(16):
                        sXC.inc(sy.dma_start(
                            out=xnag_i.ap()[128 * dj:128 * (dj + 1), :],
                            in_=xnT[:, TOWN * dj:TOWN * (dj + 1)]), 16)

                if no_stats:
                    @block.gpsimd
                    def _(g):
                        g.wait_ge(sXC.h, 256)
                        sAGXc.inc(g.collective_compute(
                            "AllGather", OP.bypass,
                            replica_groups=[list(range(NC))],
                            ins=[xnag_i.ap()], outs=[xnag_o.ap()]), 1)

                if not no_stats:
                    pieces = []
                    for rr in range(2):
                        for cc in range(10):
                            r0, c0 = 128 * rr, 512 * cc
                            pieces.append((wfcT_d[r0:r0 + 128, c0:c0 + 512], 512, 0,
                                           wfag_i.ap()[r0:r0 + 128, c0:c0 + 512]))
                    for rr in range(5):
                        for cc in range(4):
                            r0, c0 = 128 * rr, 512 * cc
                            pieces.append((wprT_d[r0:r0 + 128, c0:c0 + 512], 512, 1,
                                           wpag_i.ap()[r0:r0 + 128, c0:c0 + 512]))
                    NP = len(pieces)

                    @block.gpsimd
                    def _(g):
                        for a in range(16):
                            sWQ.inc(g.dma_start(
                                out=wq_sb[:, QC * a:QC * (a + 1)],
                                in_=wq_d[128 * a:128 * (a + 1), :]), 16)
                            sWQ.inc(g.dma_start(
                                out=wkv_sb[:, 128 * a:128 * a + 64],
                                in_=wk_d[128 * a:128 * (a + 1), :]), 16)
                            sWQ.inc(g.dma_start(
                                out=wkv_sb[:, 128 * a + 64:128 * a + 128],
                                in_=wv_d[128 * a:128 * (a + 1), :]), 16)
                        for p, (src, w, mat, dst) in enumerate(pieces):
                            if p >= 2:
                                g.wait_ge(sSTv.h, p - 1)
                            sTS.inc(g.dma_start(out=tsrc[p % 2][:, 0:w], in_=src), 16)

                    @block.vector
                    def _(v):
                        for p, (src, w, mat, dst) in enumerate(pieces):
                            v.wait_ge(sTS.h, 16 * (p + 1))
                            first = (p == 0) or (pieces[p][2] != pieces[p - 1][2])
                            col = mat
                            if first:
                                sSTv.inc(v.reduce_sum(
                                    out=stat4[:, col:col + 1],
                                    in_=tsrc[p % 2][:, 0:w], axis=AX.X,
                                    apply_absolute_value=True), 1)
                                v.wait_ge(sSTv.h, sSTv.n)
                            else:
                                vf(v, v.reduce_sum(
                                    out=stat4[:, 2:3],
                                    in_=tsrc[p % 2][:, 0:w], axis=AX.X,
                                    apply_absolute_value=True))
                                ins = v.tensor_tensor(
                                    out=stat4[:, col:col + 1],
                                    in0=stat4[:, col:col + 1],
                                    in1=stat4[:, 2:3], op=OP.add)
                                sSTv.inc(ins, 1)
                                v.wait_ge(sSTv.h, sSTv.n)

                    @block.tensor
                    def _(t):
                        t.wait_ge(sSTv.h, NP)
                        sSTm.inc(t.matmul(pmisc[0:1, 0:2], onesf[:, 0:1],
                                          stat4[:, 0:2], start=True, stop=True), 1)

                    @block.vector
                    def _(v):
                        v.wait_ge(sSTm.h, 1)
                        sSTe.inc(v.tensor_copy(out=st2[0:1, 0:2],
                                               in_=pmisc[0:1, 0:2]), 1)

                    @block.gpsimd
                    def _(g):
                        g.wait_ge(sSTe.h, 1)
                        sARd.inc(g.dma_start(out=stag_i.ap(), in_=st2[0:1, 0:2]), 16)
                        g.wait_ge(sARd.h, 16)
                        sARc.inc(g.collective_compute(
                            "AllReduce", OP.add, replica_groups=[list(range(NC))],
                            ins=[stag_i.ap()], outs=[stag_o.ap()]), 1)
                        g.wait_ge(sARc.h, 1)
                        sARd.inc(g.dma_start(out=st2[0:1, 2:4], in_=stag_o.ap()), 16)
                        g.wait_ge(sARd.h, 32)
                        g.wait_ge(sXC.h, 256)
                        sAGXc.inc(g.collective_compute(
                            "AllGather", OP.bypass, replica_groups=[list(range(NC))],
                            ins=[xnag_i.ap()], outs=[xnag_o.ap()]), 1)

                    @block.vector
                    def _(v):
                        v.wait_ge(sARd.h, 32)
                        vf(v, v.tensor_scalar(out=st2[0:1, 0:2], in0=st2[0:1, 2:4],
                                              scalar1=1.0 / NWE, scalar2=1e-5,
                                              op0=OP.mult, op1=OP.max))
                        sSTe.inc(v.tensor_scalar(out=stat4[0:1, 0:2],
                                                 in0=st2[0:1, 0:2],
                                                 scalar1=0.5, scalar2=None,
                                                 op0=OP.mult), 1)
                        sSTe.inc(v.tensor_scalar(out=stat4[0:1, 2:4],
                                                 in0=st2[0:1, 0:2],
                                                 scalar1=-0.7, scalar2=None,
                                                 op0=OP.mult), 1)

                    @block.tensor
                    def _(t):
                        t.wait_ge(sSTe.h, 3)
                        sSTm.inc(t.matmul(pmisc[:, 0:4], onesf[0:1, :],
                                          stat4[0:1, 0:4], start=True, stop=True), 1)

                    @block.vector
                    def _(v):
                        v.wait_ge(sSTm.h, 2)
                        sCB.inc(v.tensor_copy(out=cb4[:, :], in_=pmisc[:, 0:4]), 1)

                    @block.gpsimd
                    def _(g):
                        g.wait_ge(sTW.h, 16 * NP)
                        g.wait_ge(sAGXc.h, 1)
                        sAGWc.inc(g.collective_compute(
                            "AllGather", OP.bypass, replica_groups=[list(range(NC))],
                            ins=[wfag_i.ap()], outs=[wfag_o.ap()]), 1)
                        g.wait_ge(sAGWc.h, 1)
                        sAGWc.inc(g.collective_compute(
                            "AllGather", OP.bypass, replica_groups=[list(range(NC))],
                            ins=[wpag_i.ap()], outs=[wpag_o.ap()]), 1)

                    @block.scalar
                    def _(sc):
                        sc.wait_ge(sCB.h, 1)
                        for pp in (0, 1):
                            if pp == 0:
                                sc.wait_ge(sSTv.h, NP)
                                sc.wait_ge(sAGXc.h, 1)
                                sc.wait_ge(sARd.h, 32)
                            src2, w2b, _, _ = pieces[pp]
                            sTS2.inc(sc.dma_start(out=tsrc[pp % 2][:, 0:w2b],
                                                  in_=src2), 16)
                        for p, (src, w, mat, dst) in enumerate(pieces):
                            sc.wait_ge(sTS2.h, 16 * (p + 1))
                            if p >= 1:
                                sc.wait_ge(sTT.h, 3 * p)
                            bias = cb4[:, 2 + mat:3 + mat]
                            sSG.inc(sc.activation(out=t1buf[:, 0:w],
                                                  in_=tsrc[p % 2][:, 0:w],
                                                  func=AF.Sign, bias=bias), 1)
                            sSG.inc(sc.activation(out=t2buf[:, 0:w],
                                                  in_=tsrc[p % 2][:, 0:w],
                                                  func=AF.Sign, bias=bias,
                                                  scale=-1.0), 1)
                            if p + 2 < NP:
                                sc.wait_ge(sSG.h, sSG.n)
                                src2, w2b, _, _ = pieces[p + 2]
                                sTS2.inc(sc.dma_start(
                                    out=tsrc[(p + 2) % 2][:, 0:w2b], in_=src2), 16)
                            if p >= 2:
                                p2 = p - 2
                                _, w2b, _, dst2 = pieces[p2]
                                sc.wait_ge(sTT.h, 3 * (p2 + 1))
                                sTW.inc(sc.dma_start(out=dst2,
                                                     in_=tstg[p2 % 2][:, 0:w2b]), 16)
                        for p2 in (NP - 2, NP - 1):
                            _, w2b, _, dst2 = pieces[p2]
                            sc.wait_ge(sTT.h, 3 * (p2 + 1))
                            sTW.inc(sc.dma_start(out=dst2,
                                                 in_=tstg[p2 % 2][:, 0:w2b]), 16)

                    @block.vector
                    def _(v):
                        for p, (src, w, mat, dst) in enumerate(pieces):
                            v.wait_ge(sSG.h, 2 * (p + 1))
                            if p >= 2:
                                v.wait_ge(sTW.h, 16 * (p - 1))
                            ins = v.tensor_tensor(out=t1buf[:, 0:w],
                                                  in0=t1buf[:, 0:w],
                                                  in1=t2buf[:, 0:w],
                                                  op=OP.subtract)
                            sTT.inc(ins, 1)
                            v.wait_ge(sTT.h, sTT.n)
                            sTT.inc(v.tensor_scalar(out=tstg[p % 2][:, 0:w],
                                                    in0=t1buf[:, 0:w],
                                                    scalar1=cb4[:, mat:mat + 1],
                                                    scalar2=None, op0=OP.mult), 2)


                if (t_xnt is not None or t_tfc is not None or t_xno is not None
                        or t_cb is not None or t_tfi is not None):
                    @block.sync
                    def _(sy):
                        if t_xno is not None:
                            sy.wait_ge(sXNT.h, 32)
                            for dj in range(16):
                                sDBG.inc(sy.dma_start(
                                    out=t_xno.ap()[128 * dj:128 * (dj + 1), :],
                                    in_=xnT[:, TOWN * dj:TOWN * (dj + 1)]), 16)
                        if t_xnt is not None:
                            sy.wait_ge(sAGXc.h, 1)
                            sDBG.inc(sy.dma_start(out=t_xnt.ap(), in_=xnag_o.ap()), 16)
                        if t_cb is not None:
                            sy.wait_ge(sCB.h, 1)
                            sDBG.inc(sy.dma_start(out=t_cb.ap()[:, 0:4],
                                                  in_=cb4[:, :]), 16)
                            sDBG.inc(sy.dma_start(out=t_cb.ap()[0:1, 4:8],
                                                  in_=st2[:, :]), 16)
                        if t_tfi is not None:
                            sy.wait_ge(sTW.h, 16 * NP)
                            sDBG.inc(sy.dma_start(out=t_tfi.ap(), in_=wfag_i.ap()), 16)
                        if t_tfc is not None:
                            sy.wait_ge(sAGWc.h, 2)
                            sDBG.inc(sy.dma_start(out=t_tfc.ap(), in_=wfag_o.ap()), 16)
                            sDBG.inc(sy.dma_start(out=t_tpr.ap(), in_=wpag_o.ap()), 16)

                if upto == "A":
                    _drain(nc, block, out_d, [c for c in (sDBG,) if c.n], [sOS])
                    return nc, taps

                pA.close()

                pC = ExitStack()
                pacc = [pC.enter_context(nc.psum_tensor(f"pacc{i}", [128, 512], F32))
                        for i in range(2)]
                pqs = [pC.enter_context(nc.psum_tensor(f"pqs{i}", [128, 512], F32))
                       for i in range(2)]
                ptv = pC.enter_context(nc.psum_tensor("ptv", [128, 4, 64], BF16))

                qe_of = lambda ts, k: 5 * ts + k
                qp_of = lambda ts, k: 4 * ts + k
                qs_of = lambda ts, k: 3 * ts + k
                t1_of = lambda ts, k: 3 * ts + k
                t2_of = lambda ts, k: 3 * ts + k
                rp_of = lambda ts, k: 3 * ts + k

                @block.sync
                def _(sy):
                    sy.wait_ge(sAGXc.h, 1)
                    for ts in range(16):
                        if ts >= 2:
                            sy.wait_ge(sQP.h, 4 * (ts - 2) + 4)
                        r, cpart = ts // 2, ts % 2
                        for a in range(16):
                            sRH.inc(sy.dma_start(
                                out=rhs_db[ts % 2][:, a, :],
                                in_=xnag_o.ap()[2048 * r + 128 * a:
                                                2048 * r + 128 * (a + 1),
                                                512 * cpart:512 * (cpart + 1)]), 16)
                        s0 = 512 * (ts % 8)
                        if ts >= 2:
                            sy.wait_ge(sRP.h, rp_of(ts - 2, 3))
                        sCS.inc(sy.dma_start(out=csb[ts % 2][:, 0, :],
                                             in_=CT_d.ap()[:, s0:s0 + 512]), 16)
                        sCS.inc(sy.dma_start(out=csb[ts % 2][:, 1, :],
                                             in_=ST_d.ap()[:, s0:s0 + 512]), 16)

                @block.tensor
                def _(t):
                    t.wait_ge(sWQ.h, sWQ.n)
                    t.wait_ge(sCB.h, 1)
                    t.wait_ge(sVC.h, 0)
                    for ts in range(16):
                        t.wait_ge(sRH.h, 256 * (ts + 1))
                        if ts >= 1:
                            t.wait_ge(sQE.h, qe_of(ts - 1, 4))
                        for d in range(16):
                            ins = t.matmul(pacc[0][:, :],
                                           wq_sb[:, QC * d:QC * d + 128],
                                           rhs_db[ts % 2][:, d, :],
                                           start=(d == 0), stop=(d == 15))
                        sQP.inc(ins, 1)
                        if ts >= 1:
                            t.wait_ge(sQE.h, qe_of(ts - 1, 5))
                        for d in range(16):
                            ins = t.matmul(pacc[1][:, :],
                                           wq_sb[:, QC * d + 128:QC * (d + 1)],
                                           rhs_db[ts % 2][:, d, :],
                                           start=(d == 0), stop=(d == 15))
                        sQP.inc(ins, 1)
                        t.wait_ge(sQE.h, qe_of(ts, 1))
                        if ts >= 1:
                            t.wait_ge(sT2.h, t2_of(ts - 1, 3))
                        sQS.inc(t.matmul(pqs[0][:, :], Pf[:, :], qstage[0][:, :],
                                         start=True, stop=True), 1)
                        t.wait_ge(sQE.h, qe_of(ts, 1))
                        for d in range(16):
                            ins = t.matmul(pacc[0][:, :],
                                           wkv_sb[:, 128 * d:128 * (d + 1)],
                                           rhs_db[ts % 2][:, d, :],
                                           start=(d == 0), stop=(d == 15))
                        sQP.inc(ins, 1)
                        t.wait_ge(sQE.h, qe_of(ts, 2))
                        if ts >= 1:
                            t.wait_ge(sT2.h, t2_of(ts - 1, 2))
                        sQS.inc(t.matmul(pqs[1][:, :], Pf[:, :], qstage[1][:, :],
                                         start=True, stop=True), 1)
                        t.wait_ge(sQE.h, qe_of(ts, 2))
                        for d in range(16):
                            ins = t.matmul(pacc[1][64:128, :],
                                           wkv_sb[:, 128 * d:128 * d + 64],
                                           rhs_db[ts % 2][:, d, :],
                                           start=(d == 0), stop=(d == 15))
                        sQP.inc(ins, 1)
                        t.wait_ge(sQE.h, qe_of(ts, 4))
                        if ts >= 1:
                            t.wait_ge(sVC.h, ts)
                        for i in range(4):
                            ins = t.transpose(ptv[:, i, :],
                                              vstage[64:128, 128 * i:128 * (i + 1)],
                                              id64[64:128, :])
                        sTV.inc(ins, 1)
                        t.wait_ge(sQE.h, qe_of(ts, 5))
                        t.wait_ge(sT2.h, t2_of(ts, 1))
                        sQS.inc(t.matmul(pqs[0][:, :], Pf[:, :], qstage[0][:, :],
                                         start=True, stop=True), 1)

                @block.scalar
                def _(sc):
                    for ts in range(16):
                        sc.wait_ge(sQP.h, qp_of(ts, 1))
                        if ts >= 1:
                            sc.wait_ge(sT1.h, t1_of(ts - 1, 3))
                            sc.wait_ge(sQS.h, qs_of(ts - 1, 3))
                            sc.wait_ge(sRP.h, rp_of(ts - 1, 3))
                        sQE.inc(sc.activation(out=qstage[0][:, :], in_=pacc[0][:, :],
                                              func=AF.Identity,
                                              bias=bq_sb[:, 0:1]), 1)
                        sc.wait_ge(sQP.h, qp_of(ts, 2))
                        if ts >= 1:
                            sc.wait_ge(sT1.h, t1_of(ts - 1, 2))
                            sc.wait_ge(sQS.h, qs_of(ts - 1, 2))
                            sc.wait_ge(sRP.h, rp_of(ts - 1, 2))
                        sQE.inc(sc.activation(out=qstage[1][:, :], in_=pacc[1][:, :],
                                              func=AF.Identity,
                                              bias=bq_sb[:, 1:2]), 1)
                        sc.wait_ge(sQP.h, qp_of(ts, 3))
                        sc.wait_ge(sT1.h, t1_of(ts, 1))
                        sc.wait_ge(sQS.h, qs_of(ts, 1))
                        sc.wait_ge(sRP.h, rp_of(ts, 1))
                        sQE.inc(sc.activation(out=qstage[0][0:64, :],
                                              in_=pacc[0][0:64, :],
                                              func=AF.Identity,
                                              bias=bkv_sb[0:64, :]), 1)
                        if ts >= 1:
                            sc.wait_ge(sTV.h, ts)
                        sQE.inc(sc.activation(out=vstage[64:128, :],
                                              in_=pacc[0][64:128, :],
                                              func=AF.Identity,
                                              bias=bkv_sb[64:128, :]), 1)
                        sc.wait_ge(sQP.h, qp_of(ts, 4))
                        sQE.inc(sc.activation(out=qstage[0][64:128, :],
                                              in_=pacc[1][64:128, :],
                                              func=AF.Identity,
                                              bias=bkd_sb[64:128, :]), 1)

                @block.vector
                def _(v):
                    for ts in range(16):
                        col = slice(512 * ts, 512 * (ts + 1))
                        v.wait_ge(sCS.h, 32 * (ts + 1))
                        v.wait_ge(sQE.h, qe_of(ts, 1))
                        sT1.inc(v.tensor_tensor(out=tmp1[:, :],
                                                in0=qstage[0][:, :],
                                                in1=csb[ts % 2][:, 0, :],
                                                op=OP.mult), 1)
                        v.wait_ge(sT1.h, sT1.n)
                        v.wait_ge(sQS.h, qs_of(ts, 1))
                        sT2.inc(v.tensor_tensor(out=qstage[0][:, :],
                                                in0=csb[ts % 2][:, 1, :],
                                                in1=pqs[0][:, :], op=OP.mult), 1)
                        v.wait_ge(sT2.h, sT2.n)
                        sRP.inc(v.tensor_tensor(out=QT[0][:, col], in0=tmp1[:, :],
                                                in1=qstage[0][:, :], op=OP.add), 1)
                        v.wait_ge(sQE.h, qe_of(ts, 2))
                        sT1.inc(v.tensor_tensor(out=tmp1[:, :],
                                                in0=qstage[1][:, :],
                                                in1=csb[ts % 2][:, 0, :],
                                                op=OP.mult), 1)
                        v.wait_ge(sT1.h, sT1.n)
                        v.wait_ge(sQS.h, qs_of(ts, 2))
                        sT2.inc(v.tensor_tensor(out=qstage[1][:, :],
                                                in0=csb[ts % 2][:, 1, :],
                                                in1=pqs[1][:, :], op=OP.mult), 1)
                        v.wait_ge(sT2.h, sT2.n)
                        sRP.inc(v.tensor_tensor(out=QT[1][:, col], in0=tmp1[:, :],
                                                in1=qstage[1][:, :], op=OP.add), 1)
                        v.wait_ge(sQE.h, qe_of(ts, 5))
                        sT1.inc(v.tensor_tensor(out=tmp1[:, :],
                                                in0=qstage[0][:, :],
                                                in1=csb[ts % 2][:, 0, :],
                                                op=OP.mult), 1)
                        v.wait_ge(sT1.h, sT1.n)
                        v.wait_ge(sQS.h, qs_of(ts, 3))
                        sT2.inc(v.tensor_tensor(out=qstage[0][:, :],
                                                in0=csb[ts % 2][:, 1, :],
                                                in1=pqs[0][:, :], op=OP.mult), 1)
                        v.wait_ge(sT2.h, sT2.n)
                        sRP.inc(v.tensor_tensor(out=K2[:, col], in0=tmp1[:, :],
                                                in1=qstage[0][:, :], op=OP.add), 1)
                        v.wait_ge(sTV.h, ts + 1)
                        sVC.inc(v.tensor_copy(out=Vt[:, 4 * ts:4 * ts + 4, 0:64],
                                              in_=ptv[:, :, :]), 1)

                if t_qt is not None:
                    @block.sync
                    def _(sy):
                        sy.wait_ge(sRP.h, 48)
                        sy.wait_ge(sVC.h, 16)
                        sDBG.inc(sy.dma_start(out=t_qt.ap()[0:128, :],
                                              in_=QT[0][:, :]), 16)
                        sDBG.inc(sy.dma_start(out=t_qt.ap()[128:256, :],
                                              in_=QT[1][:, :]), 16)
                        sDBG.inc(sy.dma_start(out=t_k2.ap(), in_=K2[:, :]), 16)
                        sDBG.inc(sy.dma_start(
                            out=t_vt.ap(),
                            in_=Vt.ap().rearrange("p a b -> p (a b)")), 16)

                if upto == "C":
                    _drain(nc, block, out_d, [sDBG], [sOS])
                    return nc, taps

                pC.close()
            r1.close()

            pD = ExitStack()
            psct = [pD.enter_context(nc.psum_tensor(f"psct{i}", [128, 2, 512], F32))
                    for i in range(2)]
            pav = [pD.enter_context(nc.psum_tensor(f"pav{i}", [128, 512], F32))
                   for i in range(2)]
            pbc = pD.enter_context(nc.psum_tensor("pbc", [128, 2, 512], F32))

            iters = []
            qidx = 0
            for b in range(2):
                for hp in range(2):
                    for qtc in range(8):
                        njs = 4 * qtc + 4
                        for j in range(njs):
                            iters.append((b, hp, qtc, j, njs, qidx))
                        qidx += 1
            NIT = len(iters)
            tri_pref = []
            c = 0
            for (b, hp, qtc, j, njs, qx) in iters:
                if j >= 4 * qtc:
                    c += 1
                tri_pref.append(c)

            def _av(t, it, i):
                b, hp, qtc, j, njs, qx = it
                qt0 = 4096 * b + 512 * qtc
                f0 = 128 * (j - 4 * qtc) if j >= 4 * qtc else 0
                kb = 32 * b + j
                t.wait_ge(sEXP.h, i + 1)
                if j >= 4 * qtc:
                    t.wait_ge(sTRI.h, 2 * tri_pref[i])
                if j == 0 and qx >= 1:
                    t.wait_ge(sEPI.h, 2 * qx)
                t.matmul(pav[0][0:65, f0:512], Vt[:, kb, :],
                         pstage[i % 2][:, 0, f0:512],
                         start=(j == 0), stop=(j == njs - 1))
                sAV.inc(t.matmul(pav[1][0:65, f0:512], Vt[:, kb, :],
                                 pstage[i % 2][:, 1, f0:512],
                                 start=(j == 0), stop=(j == njs - 1)), 1)

            @block.tensor
            def _(t):
                t.wait_ge(sRP.h, 48)
                t.wait_ge(sVC.h, 16)
                t.wait_ge(sQE.h, 80)
                last_of_q = {}
                for i, (b, hp, qtc, j, njs, qx) in enumerate(iters):
                    if j == njs - 1:
                        last_of_q[qx] = i
                for i, (b, hp, qtc, j, njs, qx) in enumerate(iters):
                    qt0 = 4096 * b + 512 * qtc
                    kt0 = 4096 * b + 128 * j
                    f0 = 128 * (j - 4 * qtc) if j >= 4 * qtc else 0
                    if i >= 2:
                        t.wait_ge(sEXP.h, i - 1)
                    t.matmul(psct[i % 2][:, 0, f0:512],
                             K2[0:64, kt0:kt0 + 128],
                             QT[hp][0:64, qt0 + f0:qt0 + 512],
                             start=True, stop=True)
                    sSC.inc(t.matmul(psct[i % 2][:, 1, f0:512],
                                     K2[64:128, kt0:kt0 + 128],
                                     QT[hp][64:128, qt0 + f0:qt0 + 512],
                                     start=True, stop=True), 1)
                    if i >= 1:
                        _av(t, iters[i - 1], i - 1)
                    for qx2, li in list(last_of_q.items()):
                        if li == i - 1:
                            t.wait_ge(sRC.h, 2 * qx2 + 2)
                            if qx2 >= 1:
                                t.wait_ge(sEPI.h, 2 * qx2)
                            t.matmul(pbc[0:64, 0, :], onesf[64:65, 0:64],
                                     rs_sb[64:65, 0, :], start=True, stop=True)
                            sBC.inc(t.matmul(pbc[0:64, 1, :], onesf[64:65, 0:64],
                                             rs_sb[64:65, 1, :],
                                             start=True, stop=True), 2)
                            del last_of_q[qx2]
                _av(t, iters[NIT - 1], NIT - 1)
                qx2 = iters[NIT - 1][5]
                t.wait_ge(sRC.h, 2 * qx2 + 2)
                t.wait_ge(sEPI.h, 2 * qx2)
                t.matmul(pbc[0:64, 0, :], onesf[64:65, 0:64],
                         rs_sb[64:65, 0, :], start=True, stop=True)
                sBC.inc(t.matmul(pbc[0:64, 1, :], onesf[64:65, 0:64],
                                 rs_sb[64:65, 1, :], start=True, stop=True), 2)

            @block.scalar
            def _(sc):
                for i, (b, hp, qtc, j, njs, qx) in enumerate(iters):
                    f0 = 128 * (j - 4 * qtc) if j >= 4 * qtc else 0
                    sc.wait_ge(sSC.h, i + 1)
                    if i >= 2:
                        sc.wait_ge(sAV.h, i - 1)
                    sEXP.inc(sc.activation(out=pstage[i % 2][:, :, f0:512],
                                           in_=psct[i % 2][:, :, f0:512],
                                           func=AF.Exp, scale=0.125), 1)

            @block.vector
            def _(v):
                for i, (b, hp, qtc, j, njs, qx) in enumerate(iters):
                    if j >= 4 * qtc:
                        f0 = 128 * (j - 4 * qtc)
                        v.wait_ge(sEXP.h, i + 1)
                        sTRI.inc(v.tensor_tensor(
                            out=pstage[i % 2][:, 0, f0:f0 + 128],
                            in0=pstage[i % 2][:, 0, f0:f0 + 128],
                            in1=trib[:, :], op=OP.mult), 1)
                        sTRI.inc(v.tensor_tensor(
                            out=pstage[i % 2][:, 1, f0:f0 + 128],
                            in0=pstage[i % 2][:, 1, f0:f0 + 128],
                            in1=trib[:, :], op=OP.mult), 1)
                    if j == njs - 1:
                        v.wait_ge(sAV.h, i + 1)
                        sRC.inc(v.reciprocal(out=rs_sb[64:65, 0, :],
                                             in_=pav[0][64:65, :]), 1)
                        sRC.inc(v.reciprocal(out=rs_sb[64:65, 1, :],
                                             in_=pav[1][64:65, :]), 1)
                        v.wait_ge(sBC.h, 2 * qx + 2)
                        if qx >= 2:
                            v.wait_ge(sSTG.h, 32 * (qx - 1))
                        vf(v, v.tensor_copy(out=rb_sb[0:64, 0, :],
                                            in_=pbc[0:64, 0, :]))
                        vf(v, v.tensor_copy(out=rb_sb[0:64, 1, :],
                                            in_=pbc[0:64, 1, :]))
                        sEPI.inc(v.tensor_tensor(out=stg[2 * (qx % 2)][0:64, :],
                                                 in0=pav[0][0:64, :],
                                                 in1=rb_sb[0:64, 0, :],
                                                 op=OP.mult), 1)
                        sEPI.inc(v.tensor_tensor(out=stg[2 * (qx % 2) + 1][0:64, :],
                                                 in0=pav[1][0:64, :],
                                                 in1=rb_sb[0:64, 1, :],
                                                 op=OP.mult), 1)

            @block.sync
            def _(sy):
                for qx in range(32):
                    b, hp, qtc = qx // 16, (qx // 8) % 2, qx % 8
                    qt0 = 4096 * b + 512 * qtc
                    sy.wait_ge(sEPI.h, 2 * qx + 2)
                    hA, hB = 2 * hp, 2 * hp + 1
                    sSTG.inc(sy.dma_start(
                        out=atag_i.ap()[64 * hA:64 * hA + 64, qt0:qt0 + 512],
                        in_=stg[2 * (qx % 2)][0:64, :]), 16)
                    sSTG.inc(sy.dma_start(
                        out=atag_i.ap()[64 * hB:64 * hB + 64, qt0:qt0 + 512],
                        in_=stg[2 * (qx % 2) + 1][0:64, :]), 16)

            @block.gpsimd
            def _(g):
                g.wait_ge(sSTG.h, 32 * 32)
                sAGAc.inc(g.collective_compute(
                    "AllGather", OP.bypass, replica_groups=[list(range(NC))],
                    ins=[atag_i.ap()], outs=[atag_o.ap()]), 1)

            if t_at is not None:
                @block.sync
                def _(sy):
                    sy.wait_ge(sAGAc.h, 1)
                    sDBG.inc(sy.dma_start(out=t_at.ap(), in_=atag_o.ap()), 16)

            if upto == "D":
                _drain(nc, block, out_d, [sDBG], [sOS])
                return nc, taps

            pD.close()
        qkv_scope.close()

        r2 = ExitStack()
        with r2:
            hT = r2.enter_context(nc.sbuf_tensor("hT", [128, 40 * 512], BF16))
            wpc = [r2.enter_context(nc.sbuf_tensor(f"wpc{i}", [128, 40, 256], BF16))
                   for i in range(2)]
            woc = [r2.enter_context(nc.sbuf_tensor(f"woc{i}", [128, 16, 256], BF16))
                   for i in range(2)]
            atn = r2.enter_context(nc.sbuf_tensor("atn", [128, 16, 512], BF16))
            xre = [r2.enter_context(nc.sbuf_tensor(f"xre{i}", [128, 4, 256], F32))
                   for i in range(2)]
            fsl = [r2.enter_context(nc.sbuf_tensor(f"fsl{i}", [128, 16, 256], BF16))
                   for i in range(2)]
            ost = [r2.enter_context(nc.sbuf_tensor(f"ost{i}", [128, 256], F32))
                   for i in range(4)]
            bout_sb = r2.enter_context(nc.sbuf_tensor("bout_sb", [1, D], F32))
            pEF = ExitStack()
            with pEF:
                ph = [pEF.enter_context(nc.psum_tensor(f"ph{i}", [128, 512], F32))
                      for i in range(4)]
                po = [pEF.enter_context(nc.psum_tensor(f"po{i}", [128, 256], F32))
                      for i in range(2)]

                def _fstores(sy, bx):
                    u2, d22 = bx // 8, bx % 8
                    for tt in range(4):
                        px = 4 * bx + tt
                        sy.wait_ge(sOE.h, px + 1)
                        sOS.inc(sy.dma_start(
                            out=out_d[512 * u2 + 128 * tt:
                                      512 * u2 + 128 * (tt + 1),
                                      256 * d22:256 * (d22 + 1)],
                            in_=ost[px % 4][:, :]), 16)

                @block.sync
                def _(sy):
                    sy.wait_ge(sAV.h, NIT)
                    sy.wait_ge(sEPI.h, 64)
                    sy.wait_ge(sAGWc.h, 2)
                    sy.wait_ge(sAGAc.h, 1)
                    pid = sy.snap(sy.partition_id(), min_val=0, max_val=NC - 1)
                    for u in range(2):
                        for jg in range(20):
                            jgx = 20 * u + jg
                            if jgx >= 2:
                                sy.wait_ge(sFMM.h, jgx - 1)
                            for a in range(16):
                                sFS.inc(sy.dma_start(
                                    out=fsl[jgx % 2][:, a, :],
                                    in_=wfag_o.ap()[128 * a:128 * (a + 1),
                                                    256 * jg:256 * (jg + 1)]), 16)
                        if u == 0:
                            sy.wait_ge(sAGAc.h, 1)
                        else:
                            sy.wait_ge(sOP.h, 32)
                        coff = pid * TOWN + 512 * u
                        sPC.inc(sy.dma_start(
                            out=atn[:, :, :],
                            in_=atag_o.ap().rearrange(
                                "(a p) t -> p a t", p=128)[:, :, ds(coff, 512)]), 16)
                        for d2 in range(8):
                            bx = 8 * u + d2
                            if bx >= 2:
                                sy.wait_ge(sOP.h, 4 * (bx - 1))
                            for jc in range(40):
                                sPC.inc(sy.dma_start(
                                    out=wpc[bx % 2][:, jc, :],
                                    in_=wpag_o.ap()[128 * jc:128 * (jc + 1),
                                                    256 * d2:256 * (d2 + 1)]), 16)
                            if bx >= 2:
                                sy.wait_ge(sOE.h, 4 * (bx - 1))
                            for tt in range(4):
                                sXR.inc(sy.dma_start(
                                    out=xre[bx % 2][:, tt, :],
                                    in_=x_own[512 * u + 128 * tt:
                                              512 * u + 128 * (tt + 1),
                                              256 * d2:256 * (d2 + 1)]), 16)
                            if bx >= 1:
                                _fstores(sy, bx - 1)
                    _fstores(sy, 15)
                    sy.wait_ge(sOS.h, 16 * 64)
                    if sDBG.n:
                        sy.wait_ge(sDBG.h, sDBG.n)

                @block.gpsimd
                def _(g):
                    g.wait_ge(sSTG.h, 32 * 32)
                    g.wait_ge(sAGAc.h, 1)
                    sWC.inc(g.dma_start(out=bout_sb[:, :], in_=bout_d[:, :]), 16)
                    for u in range(2):
                        for d2 in range(8):
                            bx = 8 * u + d2
                            if bx >= 2:
                                g.wait_ge(sOP.h, 4 * (bx - 1))
                            for a in range(16):
                                sWC.inc(g.dma_start(
                                    out=woc[bx % 2][:, a, :],
                                    in_=wout_d[128 * a:128 * (a + 1),
                                               256 * d2:256 * (d2 + 1)]), 16)

                @block.tensor
                def _(t):
                    for u in range(2):
                        for jg in range(20):
                            jgx = 20 * u + jg
                            t.wait_ge(sFS.h, 256 * (jgx + 1))
                            if jgx >= 2:
                                t.wait_ge(sFE.h, 2 * (jgx - 1))
                            if u == 1 and jg == 0:
                                t.wait_ge(sOP.h, 32 * u)
                            for dd in range(16):
                                insA = t.matmul(
                                    ph[2 * (jg % 2)][:, :],
                                    fsl[jgx % 2][:, dd, 0:128],
                                    xnT[:, 1024 * dd + 512 * u:
                                        1024 * dd + 512 * (u + 1)],
                                    start=(dd == 0), stop=(dd == 15))
                                insB = t.matmul(
                                    ph[2 * (jg % 2) + 1][:, :],
                                    fsl[jgx % 2][:, dd, 128:256],
                                    xnT[:, 1024 * dd + 512 * u:
                                        1024 * dd + 512 * (u + 1)],
                                    start=(dd == 0), stop=(dd == 15))
                            sFMM.inc(insB, 1)
                        for d2 in range(8):
                            bx = 8 * u + d2
                            t.wait_ge(sPC.h, 16 * (u + 1) + 640 * (bx + 1))
                            t.wait_ge(sWC.h, 16 + 256 * (bx + 1))
                            if bx == 0:
                                t.wait_ge(sFH.h, 40)
                            elif d2 == 0:
                                t.wait_ge(sFH.h, 80)
                            for tt in range(4):
                                px = 4 * bx + tt
                                if px >= 2:
                                    t.wait_ge(sOE.h, px - 1)
                                first = True
                                for jc in range(40):
                                    ins = t.matmul(
                                        po[px % 2][:, :],
                                        hT[:, 512 * jc + 128 * tt:
                                           512 * jc + 128 * (tt + 1)],
                                        wpc[bx % 2][:, jc, :],
                                        start=first, stop=False)
                                    first = False
                                for dc in range(16):
                                    t.matmul(po[px % 2][:, :],
                                             atn[:, dc, 128 * tt:128 * (tt + 1)],
                                             woc[bx % 2][:, dc, :],
                                             start=False, stop=False)
                                sOP.inc(t.matmul(
                                    po[px % 2][:, :], onesf[0:1, 0:128],
                                    bout_sb[0:1, 256 * d2:256 * (d2 + 1)],
                                    start=False, stop=True), 1)

                @block.scalar
                def _(sc):
                    for u in range(2):
                        for jg in range(20):
                            jgx = 20 * u + jg
                            sc.wait_ge(sFMM.h, jgx + 1)
                            if jgx >= 2:
                                sc.wait_ge(sFH.h, 2 * (jgx - 1))
                            sFE.inc(sc.activation(out=pstage[jg % 2][:, 0, :],
                                                  in_=ph[2 * (jg % 2)][:, :],
                                                  func=AF.Relu), 1)
                            sFE.inc(sc.activation(out=pstage[jg % 2][:, 1, :],
                                                  in_=ph[2 * (jg % 2) + 1][:, :],
                                                  func=AF.Relu), 1)

                @block.vector
                def _(v):
                    for u in range(2):
                        for jg in range(20):
                            jgx = 20 * u + jg
                            v.wait_ge(sFE.h, 2 * (jgx + 1))
                            if u == 1 and jg == 0:
                                v.wait_ge(sOP.h, 32)
                            jc = 2 * jg
                            sFH.inc(v.tensor_tensor(
                                out=hT[:, 512 * jc:512 * (jc + 1)],
                                in0=pstage[jg % 2][:, 0, :],
                                in1=pstage[jg % 2][:, 0, :],
                                op=OP.mult), 1)
                            sFH.inc(v.tensor_tensor(
                                out=hT[:, 512 * (jc + 1):512 * (jc + 2)],
                                in0=pstage[jg % 2][:, 1, :],
                                in1=pstage[jg % 2][:, 1, :], op=OP.mult), 1)
                        for d2 in range(8):
                            bx = 8 * u + d2
                            for tt in range(4):
                                px = 4 * bx + tt
                                v.wait_ge(sOP.h, px + 1)
                                v.wait_ge(sXR.h, 64 * (bx + 1))
                                if px >= 4:
                                    v.wait_ge(sOS.h, 16 * (px - 3))
                                sOE.inc(v.tensor_tensor(out=ost[px % 4][:, :],
                                                        in0=po[px % 2][:, :],
                                                        in1=xre[bx % 2][:, tt, :],
                                                        op=OP.add), 1)


    return nc, taps


def _drain(nc, block, out_d, dbg_ctrs, out_ctrs):
    @block.sync
    def _(sy):
        for c in dbg_ctrs:
            if c.n:
                sy.wait_ge(c.h, c.n)


def _host_inputs(L0_attn_x, L0_attn_Wqkv, L0_attn_bqkv, L0_attn_Wout,
                 L0_attn_bout, L0_ffn_w_fc, L0_ffn_w_proj):
    x = np.ascontiguousarray(np.asarray(L0_attn_x, dtype=np.float32).reshape(T, D))
    Wqkv = np.asarray(L0_attn_Wqkv, dtype=np.float32)
    bqkv = np.asarray(L0_attn_bqkv, dtype=np.float32)
    Wout = np.ascontiguousarray(np.asarray(L0_attn_Wout, dtype=np.float32))
    bout = np.ascontiguousarray(np.asarray(L0_attn_bout, dtype=np.float32)
                                .reshape(1, D))
    wfcT = np.ascontiguousarray(np.asarray(L0_ffn_w_fc, dtype=np.float32).T)
    wprT = np.ascontiguousarray(np.asarray(L0_ffn_w_proj, dtype=np.float32).T)
    maps = []
    for c in range(NC):
        qs = slice(QC * c, QC * (c + 1))
        ks = slice(HQ * HD + HD * c, HQ * HD + HD * (c + 1))
        vs = slice((HQ + HKV) * HD + HD * c, (HQ + HKV) * HD + HD * (c + 1))
        bk = bqkv[ks]
        bv = bqkv[vs]
        bkv = np.ascontiguousarray(np.concatenate([bk, bv]).reshape(128, 1))
        bkd = np.ascontiguousarray(np.concatenate([np.zeros(64, np.float32), bk])
                                   .reshape(128, 1))
        maps.append({
            "x_own": np.ascontiguousarray(x[TOWN * c:TOWN * (c + 1)]),
            "wq": np.ascontiguousarray(Wqkv[:, qs]),
            "wk": np.ascontiguousarray(Wqkv[:, ks]),
            "wv": np.ascontiguousarray(Wqkv[:, vs]),
            "bq": np.ascontiguousarray(bqkv[qs].reshape(QC, 1)),
            "bkv": bkv,
            "bkd": bkd,
            "wout": Wout,
            "bout": bout,
            "wfcT_sh": np.ascontiguousarray(wfcT[(D // NC) * c:(D // NC) * (c + 1)]),
            "wprT_sh": np.ascontiguousarray(wprT[(FF // NC) * c:(FF // NC) * (c + 1)]),
        })
    return maps


def kernel(L0_attn_x, L0_attn_Wqkv, L0_attn_bqkv, L0_attn_Wout, L0_attn_bout,
           L0_ffn_w_fc, L0_ffn_w_proj):
    maps = _host_inputs(L0_attn_x, L0_attn_Wqkv, L0_attn_bqkv, L0_attn_Wout,
                        L0_attn_bout, L0_ffn_w_fc, L0_ffn_w_proj)
    nc, _ = build_program()
    res = run_bass_kernel_spmd(nc, maps, core_ids=list(range(NC)))
    out = np.concatenate([res.results[c]["out"] for c in range(NC)], axis=0)
    return np.ascontiguousarray(out.reshape(B, S, D))


# version 05
# speedup vs baseline: 1.0610x; 1.0610x over previous
import numpy as np
import ml_dtypes
from contextlib import ExitStack

import concourse.bass as bass
import concourse.mybir as mybir
from concourse.bass_utils import run_bass_kernel_spmd

F32 = mybir.dt.float32
BF16 = mybir.dt.bfloat16
AF = mybir.ActivationFunctionType
OP = mybir.AluOpType
AX = mybir.AxisListType
BF = ml_dtypes.bfloat16
ds = bass.ds

B, S, D = 2, 4096, 2048
T = B * S
NC = 8
TOWN = T // NC
HQ, HKV, HD, ROT = 32, 8, 64, 32
FF = 5120
NQH = HQ // NC
QC = NQH * HD
NWE = FF * D


class Ctr:
    def __init__(self, nc, name):
        self.h = nc.alloc_semaphore(name)
        self.n = 0

    def inc(self, instr, k=1):
        instr.then_inc(self.h, k)
        self.n += k
        return self.n


def _rope_tables():
    inv = 1.0 / (10000.0 ** (np.arange(0, ROT, 2, dtype=np.float64) / ROT))
    fr = np.outer(np.arange(S, dtype=np.float64), inv)
    cos = np.cos(fr).astype(np.float32)
    sin = np.sin(fr).astype(np.float32)
    Ct = np.ones((128, S), np.float32)
    St = np.zeros((128, S), np.float32)
    for base in (0, 64):
        Ct[base + 0:base + 16] = cos.T
        Ct[base + 16:base + 32] = cos.T
        St[base + 0:base + 16] = -sin.T
        St[base + 16:base + 32] = sin.T
    return Ct, St


def _perm128():
    Pm = np.zeros((128, 128), np.float32)
    for p in range(128):
        g, r = divmod(p, 32)
        Pm[32 * g + ((r + 16) % 32), p] = 1.0
    return Pm


def build_program(debug_taps=(), upto="full", no_stats=False):
    nc = bass.Bass("TRN2", target_bir_lowering=False, debug=False,
                   detect_race_conditions=False)

    x_own = nc.dram_tensor("x_own", [TOWN, D], F32, kind="ExternalInput").ap()
    wq_d = nc.dram_tensor("wq", [D, QC], F32, kind="ExternalInput").ap()
    wk_d = nc.dram_tensor("wk", [D, HD], F32, kind="ExternalInput").ap()
    wv_d = nc.dram_tensor("wv", [D, HD], F32, kind="ExternalInput").ap()
    bq_d = nc.dram_tensor("bq", [QC, 1], F32, kind="ExternalInput").ap()
    bkv_d = nc.dram_tensor("bkv", [128, 1], F32, kind="ExternalInput").ap()
    bkd_d = nc.dram_tensor("bkd", [128, 1], F32, kind="ExternalInput").ap()
    wout_d = nc.dram_tensor("wout", [D, D], F32, kind="ExternalInput").ap()
    bout_d = nc.dram_tensor("bout", [1, D], F32, kind="ExternalInput").ap()
    wfcT_d = nc.dram_tensor("wfcT_sh", [D // NC, FF], F32, kind="ExternalInput").ap()
    wprT_d = nc.dram_tensor("wprT_sh", [FF // NC, D], F32, kind="ExternalInput").ap()
    out_d = nc.dram_tensor("out", [TOWN, D], F32, kind="ExternalOutput").ap()

    xnag_i = nc.dram_tensor("xnag_i", [D, TOWN], BF16)
    xnag_o = nc.dram_tensor("xnag_o", [NC * D, TOWN], BF16, addr_space="Shared")
    stag_i = nc.dram_tensor("stag_i", [1, 2], F32)
    stag_o = nc.dram_tensor("stag_o", [1, 2], F32)
    wfag_i = nc.dram_tensor("wfag_i", [D // NC, FF], BF16)
    wfag_o = nc.dram_tensor("wfag_o", [D, FF], BF16, addr_space="Shared")
    wpag_i = nc.dram_tensor("wpag_i", [FF // NC, D], BF16)
    wpag_o = nc.dram_tensor("wpag_o", [FF, D], BF16, addr_space="Shared")
    atag_i = nc.dram_tensor("atag_i", [QC, T], BF16)
    atag_o = nc.dram_tensor("atag_o", [D, T], BF16, addr_space="Shared")

    Cnp, Snp = _rope_tables()
    CT_d = nc.inline_tensor(Cnp, name="CTc")
    ST_d = nc.inline_tensor(Snp, name="STc")
    P_d = nc.inline_tensor(_perm128(), name="Pmc")
    tri_np = (np.arange(128)[None, :] >= np.arange(128)[:, None]).astype(BF)
    tri_d = nc.inline_tensor(tri_np, name="tric")
    ones_d = nc.inline_tensor(np.ones((128, 128), np.float32), name="onesc")
    id_d = nc.inline_tensor(np.eye(128, dtype=BF), name="idc")
    id64_np = np.zeros((128, 64), BF)
    id64_np[64:128] = np.eye(64, dtype=BF)
    id64_d = nc.inline_tensor(id64_np, name="id64c")

    taps = {}

    def tap(name, shape, dtype):
        if name in debug_taps:
            taps[name] = nc.dram_tensor("dbg_" + name, shape, dtype,
                                        kind="ExternalOutput")
        return taps.get(name)

    t_xnt = tap("xnt", [NC * D, TOWN], BF16)
    t_xno = tap("xno", [D, TOWN], BF16)
    t_tfc = tap("tfc", [D, FF], BF16)
    t_tpr = tap("tpr", [FF, D], BF16)
    t_qt = tap("qt", [QC, T], BF16)
    t_k2 = tap("k2", [128, T], BF16)
    t_vt = tap("vt", [128, 64 * 65], BF16)
    t_at = tap("at", [D, T], BF16)
    t_cb = tap("cb", [128, 8], F32)
    t_tfi = tap("tfi", [D // NC, FF], BF16)

    Cn = lambda name: Ctr(nc, name)
    sX, sSQ, sVA, sSD, sNM, sXN = (Cn("sX"), Cn("sSQ"), Cn("sVA"), Cn("sSD"),
                                   Cn("sNM"), Cn("sXN"))
    sTR, sXNT, sXC, sLDC, sWQ = (Cn("sTR"), Cn("sXNT"), Cn("sXC"), Cn("sLDC"),
                                 Cn("sWQ"))
    sTS, sSTv, sSTm, sSTe, sARd = (Cn("sTS"), Cn("sSTv"), Cn("sSTm"),
                                   Cn("sSTe"), Cn("sARd"))
    sCB, sSG, sTT, sTW = Cn("sCB"), Cn("sSG"), Cn("sTT"), Cn("sTW")
    sARc, sAGXc, sAGWc, sAGAc = (Cn("sARc"), Cn("sAGXc"), Cn("sAGWc"),
                                 Cn("sAGAc"))
    sRH, sCS, sQP, sQE, sQS = Cn("sRH"), Cn("sCS"), Cn("sQP"), Cn("sQE"), Cn("sQS")
    sT1, sT2, sRP, sTC, sTV, sVC = (Cn("sT1"), Cn("sT2"), Cn("sRP"), Cn("sTC"),
                                    Cn("sTV"), Cn("sVC"))
    sSC, sEXP, sTRI, sAV, sRC, sBC, sEPI, sSTG = (
        Cn("sSC"), Cn("sEXP"), Cn("sTRI"), Cn("sAV"), Cn("sRC"), Cn("sBC"),
        Cn("sEPI"), Cn("sSTG"))
    sFS, sFMM, sFE, sFH = Cn("sFS"), Cn("sFMM"), Cn("sFE"), Cn("sFH")
    sPC, sWC, sXR, sOP, sOE, sOS = (Cn("sPC"), Cn("sWC"), Cn("sXR"), Cn("sOP"),
                                    Cn("sOE"), Cn("sOS"))
    sDBG = Cn("sDBG")
    sTS2 = Cn("sTS2")
    sFNC = Cn("sFNC")

    def vf(v, ins):
        sFNC.inc(ins, 1)
        v.wait_ge(sFNC.h, sFNC.n)

    est = ExitStack()
    with est:
        xnT = est.enter_context(nc.sbuf_tensor("xnT", [128, 16 * TOWN], BF16))
        Pf = est.enter_context(nc.sbuf_tensor("Pf", [128, 128], F32))
        trib = est.enter_context(nc.sbuf_tensor("trib", [128, 128], BF16))
        onesf = est.enter_context(nc.sbuf_tensor("onesf", [128, 128], F32))
        idb = est.enter_context(nc.sbuf_tensor("idb", [128, 128], BF16))
        id64 = est.enter_context(nc.sbuf_tensor("id64", [128, 64], BF16))
        bq_sb = est.enter_context(nc.sbuf_tensor("bq_sb", [128, 2], F32))
        bkv_sb = est.enter_context(nc.sbuf_tensor("bkv_sb", [128, 1], F32))
        bkd_sb = est.enter_context(nc.sbuf_tensor("bkd_sb", [128, 1], F32))
        cb4 = est.enter_context(nc.sbuf_tensor("cb4", [128, 4], F32))
        stat4 = est.enter_context(nc.sbuf_tensor("stat4", [128, 4], F32))
        st2 = est.enter_context(nc.sbuf_tensor("st2", [1, 4], F32))
        sln = est.enter_context(nc.sbuf_tensor("sln", [128, 16], F32))
        qstage = [est.enter_context(nc.sbuf_tensor(f"qstage{i}", [128, 512], F32))
                  for i in range(2)]
        tmp1 = est.enter_context(nc.sbuf_tensor("tmp1", [128, 512], F32))
        vstage = est.enter_context(nc.sbuf_tensor("vstage", [128, 512], BF16))
        pstage = [est.enter_context(nc.sbuf_tensor(f"pstage{i}", [128, 2, 512], BF16))
                  for i in range(2)]
        stg = [est.enter_context(nc.sbuf_tensor(f"stg{i}", [64, 512], BF16))
               for i in range(4)]
        rs_sb = est.enter_context(nc.sbuf_tensor("rs_sb", [128, 2, 512], F32))
        rb_sb = est.enter_context(nc.sbuf_tensor("rb_sb", [128, 2, 512], F32))

        block = est.enter_context(nc.Block())

        @block.gpsimd
        def _(g):
            for dst, src in ((Pf[:, :], P_d.ap()), (trib[:, :], tri_d.ap()),
                             (onesf[:, :], ones_d.ap()), (idb[:, :], id_d.ap()),
                             (id64[:, :], id64_d.ap()),
                             (bq_sb[:, 0:1], bq_d[0:128, :]),
                             (bq_sb[:, 1:2], bq_d[128:256, :]),
                             (bkv_sb[:, :], bkv_d[:, :]),
                             (bkd_sb[:, :], bkd_d[:, :])):
                sLDC.inc(g.dma_start(out=dst, in_=src), 16)

        qkv_scope = ExitStack()
        with qkv_scope:
            QT = [qkv_scope.enter_context(nc.sbuf_tensor(f"QT{i}", [128, T], BF16))
                  for i in range(2)]
            K2 = qkv_scope.enter_context(nc.sbuf_tensor("K2", [128, T], BF16))
            Vt = qkv_scope.enter_context(nc.sbuf_tensor("Vt", [128, 64, 65], BF16))

            r1 = ExitStack()
            with r1:
                x_db = [r1.enter_context(nc.sbuf_tensor(f"x_db{i}", [128, D], F32))
                        for i in range(2)]
                xn_db = [r1.enter_context(nc.sbuf_tensor(f"xn_db{i}", [128, D], BF16))
                         for i in range(2)]
                sqdump = r1.enter_context(nc.sbuf_tensor("sqdump", [128, D], BF16))
                tsrc = [r1.enter_context(nc.sbuf_tensor(f"tsrc{i}", [128, 512], F32))
                        for i in range(2)]
                t1buf = r1.enter_context(nc.sbuf_tensor("t1buf", [128, 512], F32))
                t2buf = r1.enter_context(nc.sbuf_tensor("t2buf", [128, 512], F32))
                tstg = [r1.enter_context(nc.sbuf_tensor(f"tstg{i}", [128, 512], BF16))
                        for i in range(2)]
                csb = [r1.enter_context(nc.sbuf_tensor(f"csb{i}", [128, 2, 512], F32))
                       for i in range(2)]
                rhs_db = [r1.enter_context(
                    nc.sbuf_tensor(f"rhs{i}", [128, 16, 512], BF16)) for i in range(2)]
                wq_sb = r1.enter_context(nc.sbuf_tensor("wq_sb", [128, 16 * QC], BF16))
                wkv_sb = r1.enter_context(nc.sbuf_tensor("wkv_sb", [128, 16 * 128], BF16))

                pA = ExitStack()
                ptA = [pA.enter_context(nc.psum_tensor(f"ptA{i}", [128, 4, 128], BF16))
                       for i in range(2)]
                pmisc = pA.enter_context(nc.psum_tensor("pmisc", [128, 512], F32))

                @block.sync
                def _(sy):
                    for ti in range(8):
                        if ti >= 2:
                            sy.wait_ge(sXN.h, ti - 1)
                        sX.inc(sy.dma_start(
                            out=x_db[ti % 2][:, :],
                            in_=x_own[128 * ti:128 * (ti + 1), :]), 16)

                @block.scalar
                def _(sc):
                    for ti in range(9):
                        if ti < 8:
                            sc.wait_ge(sX.h, 16 * (ti + 1))
                            c = 8 * (ti % 2)
                            sSQ.inc(sc.activation(
                                out=sqdump[:, :], in_=x_db[ti % 2][:, :],
                                func=AF.Square, accum_out=sln[:, c + 4:c + 5]), 1)
                        if ti >= 1:
                            tj = ti - 1
                            c = 8 * (tj % 2)
                            sc.wait_ge(sVA.h, tj + 1)
                            sSD.inc(sc.activation(
                                out=sln[:, c + 6:c + 7], in_=sln[:, c + 5:c + 6],
                                func=AF.Sqrt), 1)
                            sc.wait_ge(sNM.h, tj + 1)
                            if tj >= 2:
                                sc.wait_ge(sTR.h, 4 * tj - 4)
                            sXN.inc(sc.activation(
                                out=xn_db[tj % 2][:, :], in_=x_db[tj % 2][:, :],
                                func=AF.Identity, bias=sln[:, c + 7:c + 8],
                                scale=sln[:, c + 6:c + 7]), 1)

                @block.vector
                def _(v):
                    inv_d = 1.0 / D
                    for ti in range(10):
                        if ti < 8:
                            c = 8 * (ti % 2)
                            v.wait_ge(sX.h, 16 * (ti + 1))
                            v.wait_ge(sSQ.h, ti + 1)
                            if ti >= 2:
                                v.wait_ge(sXN.h, ti - 1)
                            vf(v, v.reduce_sum(out=sln[:, c + 0:c + 1],
                                                in_=x_db[ti % 2][:, :], axis=AX.X))
                            vf(v, v.tensor_scalar(out=sln[:, c + 1:c + 2],
                                                  in0=sln[:, c + 0:c + 1],
                                                  scalar1=inv_d, scalar2=None,
                                                  op0=OP.mult))
                            vf(v, v.tensor_tensor(out=sln[:, c + 2:c + 3],
                                                  in0=sln[:, c + 1:c + 2],
                                                  in1=sln[:, c + 1:c + 2],
                                                  op=OP.mult))
                            vf(v, v.tensor_scalar(out=sln[:, c + 3:c + 4],
                                                  in0=sln[:, c + 4:c + 5],
                                                  scalar1=inv_d, scalar2=None,
                                                  op0=OP.mult))
                            vf(v, v.tensor_tensor(out=sln[:, c + 3:c + 4],
                                                  in0=sln[:, c + 3:c + 4],
                                                  in1=sln[:, c + 2:c + 3],
                                                  op=OP.subtract))
                            vf(v, v.tensor_scalar(out=sln[:, c + 3:c + 4],
                                                  in0=sln[:, c + 3:c + 4],
                                                  scalar1=1e-5, scalar2=None,
                                                  op0=OP.add))
                            sVA.inc(v.reciprocal(out=sln[:, c + 5:c + 6],
                                                 in_=sln[:, c + 3:c + 4]), 1)
                        if ti >= 1 and ti <= 8:
                            tj = ti - 1
                            c = 8 * (tj % 2)
                            v.wait_ge(sSD.h, tj + 1)
                            sNM.inc(v.tensor_scalar(
                                out=sln[:, c + 7:c + 8], in0=sln[:, c + 1:c + 2],
                                scalar1=sln[:, c + 6:c + 7], scalar2=-1.0,
                                op0=OP.mult, op1=OP.mult), 1)
                        if ti >= 2:
                            te2 = ti - 2
                            for gj in range(4):
                                G = 4 * te2 + gj
                                v.wait_ge(sTR.h, G + 1)
                                dst = xnT.ap().rearrange(
                                    "p (dj t) -> p dj t", dj=16)[
                                    :, 4 * gj:4 * gj + 4,
                                    128 * te2:128 * (te2 + 1)]
                                sXNT.inc(v.tensor_copy(out=dst,
                                                       in_=ptA[G % 2][:, :, :]), 1)
                    v.memset(Vt[:, :, 64:65], 1.0)

                @block.tensor
                def _(t):
                    t.wait_ge(sLDC.h, sLDC.n)
                    for ti in range(8):
                        t.wait_ge(sXN.h, ti + 1)
                        for gj in range(4):
                            G = 4 * ti + gj
                            if G >= 2:
                                t.wait_ge(sXNT.h, G - 1)
                            for i in range(4):
                                dj = 4 * gj + i
                                ins = t.transpose(
                                    ptA[G % 2][:, i, :],
                                    xn_db[ti % 2][:, 128 * dj:128 * (dj + 1)],
                                    idb[:, :])
                            sTR.inc(ins, 1)

                @block.sync
                def _(sy):
                    sy.wait_ge(sXNT.h, 32)
                    for dj in range(16):
                        sXC.inc(sy.dma_start(
                            out=xnag_i.ap()[128 * dj:128 * (dj + 1), :],
                            in_=xnT[:, TOWN * dj:TOWN * (dj + 1)]), 16)

                if no_stats:
                    @block.gpsimd
                    def _(g):
                        g.wait_ge(sXC.h, 256)
                        sAGXc.inc(g.collective_compute(
                            "AllGather", OP.bypass,
                            replica_groups=[list(range(NC))],
                            ins=[xnag_i.ap()], outs=[xnag_o.ap()]), 1)

                if not no_stats:
                    pieces = []
                    for rr in range(2):
                        for cc in range(10):
                            r0, c0 = 128 * rr, 512 * cc
                            pieces.append((wfcT_d[r0:r0 + 128, c0:c0 + 512], 512, 0,
                                           wfag_i.ap()[r0:r0 + 128, c0:c0 + 512]))
                    for rr in range(5):
                        for cc in range(4):
                            r0, c0 = 128 * rr, 512 * cc
                            pieces.append((wprT_d[r0:r0 + 128, c0:c0 + 512], 512, 1,
                                           wpag_i.ap()[r0:r0 + 128, c0:c0 + 512]))
                    NP = len(pieces)

                    @block.gpsimd
                    def _(g):
                        for p, (src, w, mat, dst) in enumerate(pieces):
                            if p >= 2:
                                g.wait_ge(sSTv.h, p - 1)
                            sTS.inc(g.dma_start(out=tsrc[p % 2][:, 0:w], in_=src), 16)
                        for a in range(16):
                            sWQ.inc(g.dma_start(
                                out=wq_sb[:, QC * a:QC * (a + 1)],
                                in_=wq_d[128 * a:128 * (a + 1), :]), 16)
                            sWQ.inc(g.dma_start(
                                out=wkv_sb[:, 128 * a:128 * a + 64],
                                in_=wk_d[128 * a:128 * (a + 1), :]), 16)
                            sWQ.inc(g.dma_start(
                                out=wkv_sb[:, 128 * a + 64:128 * a + 128],
                                in_=wv_d[128 * a:128 * (a + 1), :]), 16)

                    @block.vector
                    def _(v):
                        for p, (src, w, mat, dst) in enumerate(pieces):
                            v.wait_ge(sTS.h, 16 * (p + 1))
                            first = (p == 0) or (pieces[p][2] != pieces[p - 1][2])
                            col = mat
                            if first:
                                sSTv.inc(v.reduce_sum(
                                    out=stat4[:, col:col + 1],
                                    in_=tsrc[p % 2][:, 0:w], axis=AX.X,
                                    apply_absolute_value=True), 1)
                                v.wait_ge(sSTv.h, sSTv.n)
                            else:
                                vf(v, v.reduce_sum(
                                    out=stat4[:, 2:3],
                                    in_=tsrc[p % 2][:, 0:w], axis=AX.X,
                                    apply_absolute_value=True))
                                ins = v.tensor_tensor(
                                    out=stat4[:, col:col + 1],
                                    in0=stat4[:, col:col + 1],
                                    in1=stat4[:, 2:3], op=OP.add)
                                sSTv.inc(ins, 1)
                                v.wait_ge(sSTv.h, sSTv.n)

                    @block.tensor
                    def _(t):
                        t.wait_ge(sSTv.h, NP)
                        sSTm.inc(t.matmul(pmisc[0:1, 0:2], onesf[:, 0:1],
                                          stat4[:, 0:2], start=True, stop=True), 1)

                    @block.vector
                    def _(v):
                        v.wait_ge(sSTm.h, 1)
                        sSTe.inc(v.tensor_copy(out=st2[0:1, 0:2],
                                               in_=pmisc[0:1, 0:2]), 1)

                    @block.gpsimd
                    def _(g):
                        g.wait_ge(sSTe.h, 1)
                        sARd.inc(g.dma_start(out=stag_i.ap(), in_=st2[0:1, 0:2]), 16)
                        g.wait_ge(sARd.h, 16)
                        sARc.inc(g.collective_compute(
                            "AllReduce", OP.add, replica_groups=[list(range(NC))],
                            ins=[stag_i.ap()], outs=[stag_o.ap()]), 1)
                        g.wait_ge(sARc.h, 1)
                        sARd.inc(g.dma_start(out=st2[0:1, 2:4], in_=stag_o.ap()), 16)
                        g.wait_ge(sARd.h, 32)
                        g.wait_ge(sXC.h, 256)
                        sAGXc.inc(g.collective_compute(
                            "AllGather", OP.bypass, replica_groups=[list(range(NC))],
                            ins=[xnag_i.ap()], outs=[xnag_o.ap()]), 1)

                    @block.vector
                    def _(v):
                        v.wait_ge(sARd.h, 32)
                        vf(v, v.tensor_scalar(out=st2[0:1, 0:2], in0=st2[0:1, 2:4],
                                              scalar1=1.0 / NWE, scalar2=1e-5,
                                              op0=OP.mult, op1=OP.max))
                        sSTe.inc(v.tensor_scalar(out=stat4[0:1, 0:2],
                                                 in0=st2[0:1, 0:2],
                                                 scalar1=0.5, scalar2=None,
                                                 op0=OP.mult), 1)
                        sSTe.inc(v.tensor_scalar(out=stat4[0:1, 2:4],
                                                 in0=st2[0:1, 0:2],
                                                 scalar1=-0.7, scalar2=None,
                                                 op0=OP.mult), 1)

                    @block.tensor
                    def _(t):
                        t.wait_ge(sSTe.h, 3)
                        sSTm.inc(t.matmul(pmisc[:, 0:4], onesf[0:1, :],
                                          stat4[0:1, 0:4], start=True, stop=True), 1)

                    @block.vector
                    def _(v):
                        v.wait_ge(sSTm.h, 2)
                        sCB.inc(v.tensor_copy(out=cb4[:, :], in_=pmisc[:, 0:4]), 1)

                    @block.gpsimd
                    def _(g):
                        g.wait_ge(sTW.h, 16 * NP)
                        g.wait_ge(sAGXc.h, 1)
                        sAGWc.inc(g.collective_compute(
                            "AllGather", OP.bypass, replica_groups=[list(range(NC))],
                            ins=[wfag_i.ap()], outs=[wfag_o.ap()]), 1)
                        g.wait_ge(sAGWc.h, 1)
                        sAGWc.inc(g.collective_compute(
                            "AllGather", OP.bypass, replica_groups=[list(range(NC))],
                            ins=[wpag_i.ap()], outs=[wpag_o.ap()]), 1)

                    @block.scalar
                    def _(sc):
                        sc.wait_ge(sCB.h, 1)
                        for pp in (0, 1):
                            if pp == 0:
                                sc.wait_ge(sSTv.h, NP)
                                sc.wait_ge(sAGXc.h, 1)
                                sc.wait_ge(sARd.h, 32)
                            src2, w2b, _, _ = pieces[pp]
                            sTS2.inc(sc.dma_start(out=tsrc[pp % 2][:, 0:w2b],
                                                  in_=src2), 16)
                        for p, (src, w, mat, dst) in enumerate(pieces):
                            sc.wait_ge(sTS2.h, 16 * (p + 1))
                            if p >= 1:
                                sc.wait_ge(sTT.h, 3 * p)
                            bias = cb4[:, 2 + mat:3 + mat]
                            sSG.inc(sc.activation(out=t1buf[:, 0:w],
                                                  in_=tsrc[p % 2][:, 0:w],
                                                  func=AF.Sign, bias=bias), 1)
                            sSG.inc(sc.activation(out=t2buf[:, 0:w],
                                                  in_=tsrc[p % 2][:, 0:w],
                                                  func=AF.Sign, bias=bias,
                                                  scale=-1.0), 1)
                            if p + 2 < NP:
                                sc.wait_ge(sSG.h, sSG.n)
                                src2, w2b, _, _ = pieces[p + 2]
                                sTS2.inc(sc.dma_start(
                                    out=tsrc[(p + 2) % 2][:, 0:w2b], in_=src2), 16)
                            if p >= 2:
                                p2 = p - 2
                                _, w2b, _, dst2 = pieces[p2]
                                sc.wait_ge(sTT.h, 3 * (p2 + 1))
                                sTW.inc(sc.dma_start(out=dst2,
                                                     in_=tstg[p2 % 2][:, 0:w2b]), 16)
                        for p2 in (NP - 2, NP - 1):
                            _, w2b, _, dst2 = pieces[p2]
                            sc.wait_ge(sTT.h, 3 * (p2 + 1))
                            sTW.inc(sc.dma_start(out=dst2,
                                                 in_=tstg[p2 % 2][:, 0:w2b]), 16)

                    @block.vector
                    def _(v):
                        for p, (src, w, mat, dst) in enumerate(pieces):
                            v.wait_ge(sSG.h, 2 * (p + 1))
                            if p >= 2:
                                v.wait_ge(sTW.h, 16 * (p - 1))
                            ins = v.tensor_tensor(out=t1buf[:, 0:w],
                                                  in0=t1buf[:, 0:w],
                                                  in1=t2buf[:, 0:w],
                                                  op=OP.subtract)
                            sTT.inc(ins, 1)
                            v.wait_ge(sTT.h, sTT.n)
                            sTT.inc(v.tensor_scalar(out=tstg[p % 2][:, 0:w],
                                                    in0=t1buf[:, 0:w],
                                                    scalar1=cb4[:, mat:mat + 1],
                                                    scalar2=None, op0=OP.mult), 2)


                if (t_xnt is not None or t_tfc is not None or t_xno is not None
                        or t_cb is not None or t_tfi is not None):
                    @block.sync
                    def _(sy):
                        if t_xno is not None:
                            sy.wait_ge(sXNT.h, 32)
                            for dj in range(16):
                                sDBG.inc(sy.dma_start(
                                    out=t_xno.ap()[128 * dj:128 * (dj + 1), :],
                                    in_=xnT[:, TOWN * dj:TOWN * (dj + 1)]), 16)
                        if t_xnt is not None:
                            sy.wait_ge(sAGXc.h, 1)
                            sDBG.inc(sy.dma_start(out=t_xnt.ap(), in_=xnag_o.ap()), 16)
                        if t_cb is not None:
                            sy.wait_ge(sCB.h, 1)
                            sDBG.inc(sy.dma_start(out=t_cb.ap()[:, 0:4],
                                                  in_=cb4[:, :]), 16)
                            sDBG.inc(sy.dma_start(out=t_cb.ap()[0:1, 4:8],
                                                  in_=st2[:, :]), 16)
                        if t_tfi is not None:
                            sy.wait_ge(sTW.h, 16 * NP)
                            sDBG.inc(sy.dma_start(out=t_tfi.ap(), in_=wfag_i.ap()), 16)
                        if t_tfc is not None:
                            sy.wait_ge(sAGWc.h, 2)
                            sDBG.inc(sy.dma_start(out=t_tfc.ap(), in_=wfag_o.ap()), 16)
                            sDBG.inc(sy.dma_start(out=t_tpr.ap(), in_=wpag_o.ap()), 16)

                if upto == "A":
                    _drain(nc, block, out_d, [c for c in (sDBG,) if c.n], [sOS])
                    return nc, taps

                pA.close()

                pC = ExitStack()
                pacc = [pC.enter_context(nc.psum_tensor(f"pacc{i}", [128, 512], F32))
                        for i in range(2)]
                pqs = [pC.enter_context(nc.psum_tensor(f"pqs{i}", [128, 512], F32))
                       for i in range(2)]
                ptv = pC.enter_context(nc.psum_tensor("ptv", [128, 4, 64], BF16))

                qe_of = lambda ts, k: 5 * ts + k
                qp_of = lambda ts, k: 4 * ts + k
                qs_of = lambda ts, k: 3 * ts + k
                t1_of = lambda ts, k: 3 * ts + k
                t2_of = lambda ts, k: 3 * ts + k
                rp_of = lambda ts, k: 3 * ts + k

                @block.sync
                def _(sy):
                    sy.wait_ge(sAGXc.h, 1)
                    for ts in range(16):
                        if ts >= 2:
                            sy.wait_ge(sQP.h, 4 * (ts - 2) + 4)
                        r, cpart = ts // 2, ts % 2
                        sRH.inc(sy.dma_start(
                            out=rhs_db[ts % 2][:, :, :],
                            in_=xnag_o.ap()[2048 * r:2048 * (r + 1),
                                            512 * cpart:512 * (cpart + 1)].rearrange(
                                "(a p) f -> p a f", p=128)), 16)
                        s0 = 512 * (ts % 8)
                        if ts >= 2:
                            sy.wait_ge(sRP.h, rp_of(ts - 2, 3))
                        sCS.inc(sy.dma_start(out=csb[ts % 2][:, 0, :],
                                             in_=CT_d.ap()[:, s0:s0 + 512]), 16)
                        sCS.inc(sy.dma_start(out=csb[ts % 2][:, 1, :],
                                             in_=ST_d.ap()[:, s0:s0 + 512]), 16)

                @block.tensor
                def _(t):
                    t.wait_ge(sWQ.h, sWQ.n)
                    t.wait_ge(sCB.h, 1)
                    t.wait_ge(sVC.h, 0)
                    for ts in range(16):
                        t.wait_ge(sRH.h, 16 * (ts + 1))
                        if ts >= 1:
                            t.wait_ge(sQE.h, qe_of(ts - 1, 4))
                        for d in range(16):
                            ins = t.matmul(pacc[0][:, :],
                                           wq_sb[:, QC * d:QC * d + 128],
                                           rhs_db[ts % 2][:, d, :],
                                           start=(d == 0), stop=(d == 15))
                        sQP.inc(ins, 1)
                        if ts >= 1:
                            t.wait_ge(sQE.h, qe_of(ts - 1, 5))
                        for d in range(16):
                            ins = t.matmul(pacc[1][:, :],
                                           wq_sb[:, QC * d + 128:QC * (d + 1)],
                                           rhs_db[ts % 2][:, d, :],
                                           start=(d == 0), stop=(d == 15))
                        sQP.inc(ins, 1)
                        t.wait_ge(sQE.h, qe_of(ts, 1))
                        if ts >= 1:
                            t.wait_ge(sT2.h, t2_of(ts - 1, 3))
                        sQS.inc(t.matmul(pqs[0][:, :], Pf[:, :], qstage[0][:, :],
                                         start=True, stop=True), 1)
                        t.wait_ge(sQE.h, qe_of(ts, 1))
                        for d in range(16):
                            ins = t.matmul(pacc[0][:, :],
                                           wkv_sb[:, 128 * d:128 * (d + 1)],
                                           rhs_db[ts % 2][:, d, :],
                                           start=(d == 0), stop=(d == 15))
                        sQP.inc(ins, 1)
                        t.wait_ge(sQE.h, qe_of(ts, 2))
                        if ts >= 1:
                            t.wait_ge(sT2.h, t2_of(ts - 1, 2))
                        sQS.inc(t.matmul(pqs[1][:, :], Pf[:, :], qstage[1][:, :],
                                         start=True, stop=True), 1)
                        t.wait_ge(sQE.h, qe_of(ts, 2))
                        for d in range(16):
                            ins = t.matmul(pacc[1][64:128, :],
                                           wkv_sb[:, 128 * d:128 * d + 64],
                                           rhs_db[ts % 2][:, d, :],
                                           start=(d == 0), stop=(d == 15))
                        sQP.inc(ins, 1)
                        t.wait_ge(sQE.h, qe_of(ts, 4))
                        if ts >= 1:
                            t.wait_ge(sVC.h, ts)
                        for i in range(4):
                            ins = t.transpose(ptv[:, i, :],
                                              vstage[64:128, 128 * i:128 * (i + 1)],
                                              id64[64:128, :])
                        sTV.inc(ins, 1)
                        t.wait_ge(sQE.h, qe_of(ts, 5))
                        t.wait_ge(sT2.h, t2_of(ts, 1))
                        sQS.inc(t.matmul(pqs[0][:, :], Pf[:, :], qstage[0][:, :],
                                         start=True, stop=True), 1)

                @block.scalar
                def _(sc):
                    for ts in range(16):
                        sc.wait_ge(sQP.h, qp_of(ts, 1))
                        if ts >= 1:
                            sc.wait_ge(sT1.h, t1_of(ts - 1, 3))
                            sc.wait_ge(sQS.h, qs_of(ts - 1, 3))
                            sc.wait_ge(sRP.h, rp_of(ts - 1, 3))
                        sQE.inc(sc.activation(out=qstage[0][:, :], in_=pacc[0][:, :],
                                              func=AF.Identity,
                                              bias=bq_sb[:, 0:1]), 1)
                        sc.wait_ge(sQP.h, qp_of(ts, 2))
                        if ts >= 1:
                            sc.wait_ge(sT1.h, t1_of(ts - 1, 2))
                            sc.wait_ge(sQS.h, qs_of(ts - 1, 2))
                            sc.wait_ge(sRP.h, rp_of(ts - 1, 2))
                        sQE.inc(sc.activation(out=qstage[1][:, :], in_=pacc[1][:, :],
                                              func=AF.Identity,
                                              bias=bq_sb[:, 1:2]), 1)
                        sc.wait_ge(sQP.h, qp_of(ts, 3))
                        sc.wait_ge(sT1.h, t1_of(ts, 1))
                        sc.wait_ge(sQS.h, qs_of(ts, 1))
                        sc.wait_ge(sRP.h, rp_of(ts, 1))
                        sQE.inc(sc.activation(out=qstage[0][0:64, :],
                                              in_=pacc[0][0:64, :],
                                              func=AF.Identity,
                                              bias=bkv_sb[0:64, :]), 1)
                        if ts >= 1:
                            sc.wait_ge(sTV.h, ts)
                        sQE.inc(sc.activation(out=vstage[64:128, :],
                                              in_=pacc[0][64:128, :],
                                              func=AF.Identity,
                                              bias=bkv_sb[64:128, :]), 1)
                        sc.wait_ge(sQP.h, qp_of(ts, 4))
                        sQE.inc(sc.activation(out=qstage[0][64:128, :],
                                              in_=pacc[1][64:128, :],
                                              func=AF.Identity,
                                              bias=bkd_sb[64:128, :]), 1)

                @block.vector
                def _(v):
                    for ts in range(16):
                        col = slice(512 * ts, 512 * (ts + 1))
                        v.wait_ge(sCS.h, 32 * (ts + 1))
                        v.wait_ge(sQE.h, qe_of(ts, 1))
                        sT1.inc(v.tensor_tensor(out=tmp1[:, :],
                                                in0=qstage[0][:, :],
                                                in1=csb[ts % 2][:, 0, :],
                                                op=OP.mult), 1)
                        v.wait_ge(sT1.h, sT1.n)
                        v.wait_ge(sQS.h, qs_of(ts, 1))
                        sT2.inc(v.tensor_tensor(out=qstage[0][:, :],
                                                in0=csb[ts % 2][:, 1, :],
                                                in1=pqs[0][:, :], op=OP.mult), 1)
                        v.wait_ge(sT2.h, sT2.n)
                        sRP.inc(v.tensor_tensor(out=QT[0][:, col], in0=tmp1[:, :],
                                                in1=qstage[0][:, :], op=OP.add), 1)
                        v.wait_ge(sQE.h, qe_of(ts, 2))
                        sT1.inc(v.tensor_tensor(out=tmp1[:, :],
                                                in0=qstage[1][:, :],
                                                in1=csb[ts % 2][:, 0, :],
                                                op=OP.mult), 1)
                        v.wait_ge(sT1.h, sT1.n)
                        v.wait_ge(sQS.h, qs_of(ts, 2))
                        sT2.inc(v.tensor_tensor(out=qstage[1][:, :],
                                                in0=csb[ts % 2][:, 1, :],
                                                in1=pqs[1][:, :], op=OP.mult), 1)
                        v.wait_ge(sT2.h, sT2.n)
                        sRP.inc(v.tensor_tensor(out=QT[1][:, col], in0=tmp1[:, :],
                                                in1=qstage[1][:, :], op=OP.add), 1)
                        v.wait_ge(sQE.h, qe_of(ts, 5))
                        sT1.inc(v.tensor_tensor(out=tmp1[:, :],
                                                in0=qstage[0][:, :],
                                                in1=csb[ts % 2][:, 0, :],
                                                op=OP.mult), 1)
                        v.wait_ge(sT1.h, sT1.n)
                        v.wait_ge(sQS.h, qs_of(ts, 3))
                        sT2.inc(v.tensor_tensor(out=qstage[0][:, :],
                                                in0=csb[ts % 2][:, 1, :],
                                                in1=pqs[0][:, :], op=OP.mult), 1)
                        v.wait_ge(sT2.h, sT2.n)
                        sRP.inc(v.tensor_tensor(out=K2[:, col], in0=tmp1[:, :],
                                                in1=qstage[0][:, :], op=OP.add), 1)
                        v.wait_ge(sTV.h, ts + 1)
                        sVC.inc(v.tensor_copy(out=Vt[:, 4 * ts:4 * ts + 4, 0:64],
                                              in_=ptv[:, :, :]), 1)

                if t_qt is not None:
                    @block.sync
                    def _(sy):
                        sy.wait_ge(sRP.h, 48)
                        sy.wait_ge(sVC.h, 16)
                        sDBG.inc(sy.dma_start(out=t_qt.ap()[0:128, :],
                                              in_=QT[0][:, :]), 16)
                        sDBG.inc(sy.dma_start(out=t_qt.ap()[128:256, :],
                                              in_=QT[1][:, :]), 16)
                        sDBG.inc(sy.dma_start(out=t_k2.ap(), in_=K2[:, :]), 16)
                        sDBG.inc(sy.dma_start(
                            out=t_vt.ap(),
                            in_=Vt.ap().rearrange("p a b -> p (a b)")), 16)

                if upto == "C":
                    _drain(nc, block, out_d, [sDBG], [sOS])
                    return nc, taps

                pC.close()
            r1.close()

            pD = ExitStack()
            psct = [pD.enter_context(nc.psum_tensor("psct0", [128, 2, 512], F32))]
            pav = [pD.enter_context(nc.psum_tensor(f"pav{i}", [128, 512], F32))
                   for i in range(4)]
            pbc = pD.enter_context(nc.psum_tensor("pbc", [128, 2, 512], F32))

            iters = []
            qidx = 0
            for b in range(2):
                for hp in range(2):
                    for qtc in range(8):
                        njs = 4 * qtc + 4
                        for j in range(njs):
                            iters.append((b, hp, qtc, j, njs, qidx))
                        qidx += 1
            NIT = len(iters)
            tri_pref = []
            c = 0
            for (b, hp, qtc, j, njs, qx) in iters:
                if j >= 4 * qtc:
                    c += 1
                tri_pref.append(c)

            def _av(t, it, i):
                b, hp, qtc, j, njs, qx = it
                qt0 = 4096 * b + 512 * qtc
                f0 = 128 * (j - 4 * qtc) if j >= 4 * qtc else 0
                kb = 32 * b + j
                t.wait_ge(sEXP.h, i + 1)
                if j >= 4 * qtc:
                    t.wait_ge(sTRI.h, 2 * tri_pref[i])
                if j == 0 and qx >= 2:
                    t.wait_ge(sEPI.h, 2 * (qx - 1))
                pv = 2 * (qx % 2)
                t.matmul(pav[pv][0:65, f0:512], Vt[:, kb, :],
                         pstage[i % 2][:, 0, f0:512],
                         start=(j == 0), stop=(j == njs - 1))
                sAV.inc(t.matmul(pav[pv + 1][0:65, f0:512], Vt[:, kb, :],
                                 pstage[i % 2][:, 1, f0:512],
                                 start=(j == 0), stop=(j == njs - 1)), 1)

            @block.tensor
            def _(t):
                t.wait_ge(sRP.h, 48)
                t.wait_ge(sVC.h, 16)
                t.wait_ge(sQE.h, 80)
                last_of_q = {}
                for i, (b, hp, qtc, j, njs, qx) in enumerate(iters):
                    if j == njs - 1:
                        last_of_q[qx] = i
                for i, (b, hp, qtc, j, njs, qx) in enumerate(iters):
                    qt0 = 4096 * b + 512 * qtc
                    kt0 = 4096 * b + 128 * j
                    f0 = 128 * (j - 4 * qtc) if j >= 4 * qtc else 0
                    if i >= 1:
                        t.wait_ge(sEXP.h, i)
                    t.matmul(psct[0][:, 0, f0:512],
                             K2[0:64, kt0:kt0 + 128],
                             QT[hp][0:64, qt0 + f0:qt0 + 512],
                             start=True, stop=True)
                    sSC.inc(t.matmul(psct[0][:, 1, f0:512],
                                     K2[64:128, kt0:kt0 + 128],
                                     QT[hp][64:128, qt0 + f0:qt0 + 512],
                                     start=True, stop=True), 1)
                    if i >= 1:
                        _av(t, iters[i - 1], i - 1)
                    for qx2, li in list(last_of_q.items()):
                        if li == i - 1:
                            t.wait_ge(sRC.h, 2 * qx2 + 2)
                            if qx2 >= 1:
                                t.wait_ge(sEPI.h, 2 * qx2)
                            t.matmul(pbc[0:64, 0, :], onesf[64:65, 0:64],
                                     rs_sb[64:65, 0, :], start=True, stop=True)
                            sBC.inc(t.matmul(pbc[0:64, 1, :], onesf[64:65, 0:64],
                                             rs_sb[64:65, 1, :],
                                             start=True, stop=True), 2)
                            del last_of_q[qx2]
                _av(t, iters[NIT - 1], NIT - 1)
                qx2 = iters[NIT - 1][5]
                t.wait_ge(sRC.h, 2 * qx2 + 2)
                t.wait_ge(sEPI.h, 2 * qx2)
                t.matmul(pbc[0:64, 0, :], onesf[64:65, 0:64],
                         rs_sb[64:65, 0, :], start=True, stop=True)
                sBC.inc(t.matmul(pbc[0:64, 1, :], onesf[64:65, 0:64],
                                 rs_sb[64:65, 1, :], start=True, stop=True), 2)

            @block.scalar
            def _(sc):
                for i, (b, hp, qtc, j, njs, qx) in enumerate(iters):
                    f0 = 128 * (j - 4 * qtc) if j >= 4 * qtc else 0
                    sc.wait_ge(sSC.h, i + 1)
                    if i >= 2:
                        sc.wait_ge(sAV.h, i - 1)
                    sEXP.inc(sc.activation(out=pstage[i % 2][:, :, f0:512],
                                           in_=psct[0][:, :, f0:512],
                                           func=AF.Exp, scale=0.125), 1)

            @block.vector
            def _(v):
                for i, (b, hp, qtc, j, njs, qx) in enumerate(iters):
                    if j >= 4 * qtc:
                        f0 = 128 * (j - 4 * qtc)
                        v.wait_ge(sEXP.h, i + 1)
                        sTRI.inc(v.tensor_tensor(
                            out=pstage[i % 2][:, 0, f0:f0 + 128],
                            in0=pstage[i % 2][:, 0, f0:f0 + 128],
                            in1=trib[:, :], op=OP.mult), 1)
                        sTRI.inc(v.tensor_tensor(
                            out=pstage[i % 2][:, 1, f0:f0 + 128],
                            in0=pstage[i % 2][:, 1, f0:f0 + 128],
                            in1=trib[:, :], op=OP.mult), 1)
                    if j == njs - 1:
                        v.wait_ge(sAV.h, i + 1)
                        pv = 2 * (qx % 2)
                        sRC.inc(v.reciprocal(out=rs_sb[64:65, 0, :],
                                             in_=pav[pv][64:65, :]), 1)
                        sRC.inc(v.reciprocal(out=rs_sb[64:65, 1, :],
                                             in_=pav[pv + 1][64:65, :]), 1)
                        v.wait_ge(sBC.h, 2 * qx + 2)
                        if qx >= 2:
                            v.wait_ge(sSTG.h, 32 * (qx - 1))
                        vf(v, v.tensor_copy(out=rb_sb[0:64, 0, :],
                                            in_=pbc[0:64, 0, :]))
                        vf(v, v.tensor_copy(out=rb_sb[0:64, 1, :],
                                            in_=pbc[0:64, 1, :]))
                        sEPI.inc(v.tensor_tensor(out=stg[2 * (qx % 2)][0:64, :],
                                                 in0=pav[pv][0:64, :],
                                                 in1=rb_sb[0:64, 0, :],
                                                 op=OP.mult), 1)
                        sEPI.inc(v.tensor_tensor(out=stg[2 * (qx % 2) + 1][0:64, :],
                                                 in0=pav[pv + 1][0:64, :],
                                                 in1=rb_sb[0:64, 1, :],
                                                 op=OP.mult), 1)

            @block.sync
            def _(sy):
                for qx in range(32):
                    b, hp, qtc = qx // 16, (qx // 8) % 2, qx % 8
                    qt0 = 4096 * b + 512 * qtc
                    sy.wait_ge(sEPI.h, 2 * qx + 2)
                    hA, hB = 2 * hp, 2 * hp + 1
                    sSTG.inc(sy.dma_start(
                        out=atag_i.ap()[64 * hA:64 * hA + 64, qt0:qt0 + 512],
                        in_=stg[2 * (qx % 2)][0:64, :]), 16)
                    sSTG.inc(sy.dma_start(
                        out=atag_i.ap()[64 * hB:64 * hB + 64, qt0:qt0 + 512],
                        in_=stg[2 * (qx % 2) + 1][0:64, :]), 16)

            @block.gpsimd
            def _(g):
                g.wait_ge(sSTG.h, 32 * 32)
                sAGAc.inc(g.collective_compute(
                    "AllGather", OP.bypass, replica_groups=[list(range(NC))],
                    ins=[atag_i.ap()], outs=[atag_o.ap()]), 1)

            if t_at is not None:
                @block.sync
                def _(sy):
                    sy.wait_ge(sAGAc.h, 1)
                    sDBG.inc(sy.dma_start(out=t_at.ap(), in_=atag_o.ap()), 16)

            if upto == "D":
                _drain(nc, block, out_d, [sDBG], [sOS])
                return nc, taps

            pD.close()
        qkv_scope.close()

        r2 = ExitStack()
        with r2:
            hT = r2.enter_context(nc.sbuf_tensor("hT", [128, 40 * 512], BF16))
            wpc = [r2.enter_context(nc.sbuf_tensor(f"wpc{i}", [128, 40, 256], BF16))
                   for i in range(2)]
            woc = [r2.enter_context(nc.sbuf_tensor(f"woc{i}", [128, 16, 256], BF16))
                   for i in range(2)]
            atn = r2.enter_context(nc.sbuf_tensor("atn", [128, 16, 512], BF16))
            xre = [r2.enter_context(nc.sbuf_tensor(f"xre{i}", [128, 4, 256], F32))
                   for i in range(2)]
            fsl = [r2.enter_context(nc.sbuf_tensor(f"fsl{i}", [128, 16, 256], BF16))
                   for i in range(2)]
            ost = [r2.enter_context(nc.sbuf_tensor(f"ost{i}", [128, 256], F32))
                   for i in range(4)]
            bout_sb = r2.enter_context(nc.sbuf_tensor("bout_sb", [1, D], F32))
            pEF = ExitStack()
            with pEF:
                ph = [pEF.enter_context(nc.psum_tensor(f"ph{i}", [128, 512], F32))
                      for i in range(4)]
                po = [pEF.enter_context(nc.psum_tensor(f"po{i}", [128, 256], F32))
                      for i in range(2)]

                def _fstores(sy, bx):
                    u2, d22 = bx // 8, bx % 8
                    for tt in range(4):
                        px = 4 * bx + tt
                        sy.wait_ge(sOE.h, px + 1)
                        sOS.inc(sy.dma_start(
                            out=out_d[512 * u2 + 128 * tt:
                                      512 * u2 + 128 * (tt + 1),
                                      256 * d22:256 * (d22 + 1)],
                            in_=ost[px % 4][:, :]), 16)

                @block.sync
                def _(sy):
                    sy.wait_ge(sAV.h, NIT)
                    sy.wait_ge(sEPI.h, 64)
                    sy.wait_ge(sAGWc.h, 2)
                    sy.wait_ge(sAGAc.h, 1)
                    pid = sy.snap(sy.partition_id(), min_val=0, max_val=NC - 1)
                    for u in range(2):
                        for jg in range(20):
                            jgx = 20 * u + jg
                            if jgx >= 2:
                                sy.wait_ge(sFMM.h, jgx - 1)
                            sFS.inc(sy.dma_start(
                                out=fsl[jgx % 2][:, :, :],
                                in_=wfag_o.ap().rearrange(
                                    "(a p) f -> p a f",
                                    p=128)[:, :, 256 * jg:256 * (jg + 1)]), 16)
                        if u == 0:
                            sy.wait_ge(sAGAc.h, 1)
                        else:
                            sy.wait_ge(sOP.h, 32)
                        coff = pid * TOWN + 512 * u
                        sPC.inc(sy.dma_start(
                            out=atn[:, :, :],
                            in_=atag_o.ap().rearrange(
                                "(a p) t -> p a t", p=128)[:, :, ds(coff, 512)]), 16)
                        for d2 in range(8):
                            bx = 8 * u + d2
                            if bx >= 2:
                                sy.wait_ge(sOP.h, 4 * (bx - 1))
                            sPC.inc(sy.dma_start(
                                out=wpc[bx % 2][:, :, :],
                                in_=wpag_o.ap().rearrange(
                                    "(a p) f -> p a f",
                                    p=128)[:, :, 256 * d2:256 * (d2 + 1)]), 16)
                            if bx >= 2:
                                sy.wait_ge(sOE.h, 4 * (bx - 1))
                            sXR.inc(sy.dma_start(
                                out=xre[bx % 2][:, :, :],
                                in_=x_own[512 * u:512 * (u + 1),
                                          256 * d2:256 * (d2 + 1)].rearrange(
                                    "(a p) f -> p a f", p=128)), 16)
                            if bx >= 1:
                                _fstores(sy, bx - 1)
                    _fstores(sy, 15)
                    sy.wait_ge(sOS.h, 16 * 64)
                    if sDBG.n:
                        sy.wait_ge(sDBG.h, sDBG.n)

                @block.gpsimd
                def _(g):
                    g.wait_ge(sSTG.h, 32 * 32)
                    g.wait_ge(sAGAc.h, 1)
                    sWC.inc(g.dma_start(out=bout_sb[:, :], in_=bout_d[:, :]), 16)
                    for u in range(2):
                        for d2 in range(8):
                            bx = 8 * u + d2
                            if bx >= 2:
                                g.wait_ge(sOP.h, 4 * (bx - 1))
                            sWC.inc(g.dma_start(
                                out=woc[bx % 2][:, :, :],
                                in_=wout_d.rearrange(
                                    "(a p) f -> p a f",
                                    p=128)[:, :, 256 * d2:256 * (d2 + 1)]), 16)

                @block.tensor
                def _(t):
                    for u in range(2):
                        for jg in range(20):
                            jgx = 20 * u + jg
                            t.wait_ge(sFS.h, 16 * (jgx + 1))
                            if jgx >= 2:
                                t.wait_ge(sFE.h, 2 * (jgx - 1))
                            if u == 1 and jg == 0:
                                t.wait_ge(sOP.h, 32 * u)
                            for dd in range(16):
                                insA = t.matmul(
                                    ph[2 * (jg % 2)][:, :],
                                    fsl[jgx % 2][:, dd, 0:128],
                                    xnT[:, 1024 * dd + 512 * u:
                                        1024 * dd + 512 * (u + 1)],
                                    start=(dd == 0), stop=(dd == 15))
                                insB = t.matmul(
                                    ph[2 * (jg % 2) + 1][:, :],
                                    fsl[jgx % 2][:, dd, 128:256],
                                    xnT[:, 1024 * dd + 512 * u:
                                        1024 * dd + 512 * (u + 1)],
                                    start=(dd == 0), stop=(dd == 15))
                            sFMM.inc(insB, 1)
                        for d2 in range(8):
                            bx = 8 * u + d2
                            t.wait_ge(sPC.h, 16 * (u + 1) + 16 * (bx + 1))
                            t.wait_ge(sWC.h, 16 + 16 * (bx + 1))
                            if bx == 0:
                                t.wait_ge(sFH.h, 40)
                            elif d2 == 0:
                                t.wait_ge(sFH.h, 80)
                            for tt in range(4):
                                px = 4 * bx + tt
                                if px >= 2:
                                    t.wait_ge(sOE.h, px - 1)
                                first = True
                                for jc in range(40):
                                    ins = t.matmul(
                                        po[px % 2][:, :],
                                        hT[:, 512 * jc + 128 * tt:
                                           512 * jc + 128 * (tt + 1)],
                                        wpc[bx % 2][:, jc, :],
                                        start=first, stop=False)
                                    first = False
                                for dc in range(16):
                                    t.matmul(po[px % 2][:, :],
                                             atn[:, dc, 128 * tt:128 * (tt + 1)],
                                             woc[bx % 2][:, dc, :],
                                             start=False, stop=False)
                                sOP.inc(t.matmul(
                                    po[px % 2][:, :], onesf[0:1, 0:128],
                                    bout_sb[0:1, 256 * d2:256 * (d2 + 1)],
                                    start=False, stop=True), 1)

                @block.scalar
                def _(sc):
                    for u in range(2):
                        for jg in range(20):
                            jgx = 20 * u + jg
                            sc.wait_ge(sFMM.h, jgx + 1)
                            if jgx >= 2:
                                sc.wait_ge(sFH.h, 2 * (jgx - 1))
                            sFE.inc(sc.activation(out=pstage[jg % 2][:, 0, :],
                                                  in_=ph[2 * (jg % 2)][:, :],
                                                  func=AF.Relu), 1)
                            sFE.inc(sc.activation(out=pstage[jg % 2][:, 1, :],
                                                  in_=ph[2 * (jg % 2) + 1][:, :],
                                                  func=AF.Relu), 1)

                @block.vector
                def _(v):
                    for u in range(2):
                        for jg in range(20):
                            jgx = 20 * u + jg
                            v.wait_ge(sFE.h, 2 * (jgx + 1))
                            if u == 1 and jg == 0:
                                v.wait_ge(sOP.h, 32)
                            jc = 2 * jg
                            sFH.inc(v.tensor_tensor(
                                out=hT[:, 512 * jc:512 * (jc + 1)],
                                in0=pstage[jg % 2][:, 0, :],
                                in1=pstage[jg % 2][:, 0, :],
                                op=OP.mult), 1)
                            sFH.inc(v.tensor_tensor(
                                out=hT[:, 512 * (jc + 1):512 * (jc + 2)],
                                in0=pstage[jg % 2][:, 1, :],
                                in1=pstage[jg % 2][:, 1, :], op=OP.mult), 1)
                        for d2 in range(8):
                            bx = 8 * u + d2
                            for tt in range(4):
                                px = 4 * bx + tt
                                v.wait_ge(sOP.h, px + 1)
                                v.wait_ge(sXR.h, 16 * (bx + 1))
                                if px >= 4:
                                    v.wait_ge(sOS.h, 16 * (px - 3))
                                sOE.inc(v.tensor_tensor(out=ost[px % 4][:, :],
                                                        in0=po[px % 2][:, :],
                                                        in1=xre[bx % 2][:, tt, :],
                                                        op=OP.add), 1)


    return nc, taps


def _drain(nc, block, out_d, dbg_ctrs, out_ctrs):
    @block.sync
    def _(sy):
        for c in dbg_ctrs:
            if c.n:
                sy.wait_ge(c.h, c.n)


def _host_inputs(L0_attn_x, L0_attn_Wqkv, L0_attn_bqkv, L0_attn_Wout,
                 L0_attn_bout, L0_ffn_w_fc, L0_ffn_w_proj):
    x = np.ascontiguousarray(np.asarray(L0_attn_x, dtype=np.float32).reshape(T, D))
    Wqkv = np.asarray(L0_attn_Wqkv, dtype=np.float32)
    bqkv = np.asarray(L0_attn_bqkv, dtype=np.float32)
    Wout = np.ascontiguousarray(np.asarray(L0_attn_Wout, dtype=np.float32))
    bout = np.ascontiguousarray(np.asarray(L0_attn_bout, dtype=np.float32)
                                .reshape(1, D))
    wfcT = np.ascontiguousarray(np.asarray(L0_ffn_w_fc, dtype=np.float32).T)
    wprT = np.ascontiguousarray(np.asarray(L0_ffn_w_proj, dtype=np.float32).T)
    maps = []
    for c in range(NC):
        qs = slice(QC * c, QC * (c + 1))
        ks = slice(HQ * HD + HD * c, HQ * HD + HD * (c + 1))
        vs = slice((HQ + HKV) * HD + HD * c, (HQ + HKV) * HD + HD * (c + 1))
        bk = bqkv[ks]
        bv = bqkv[vs]
        bkv = np.ascontiguousarray(np.concatenate([bk, bv]).reshape(128, 1))
        bkd = np.ascontiguousarray(np.concatenate([np.zeros(64, np.float32), bk])
                                   .reshape(128, 1))
        maps.append({
            "x_own": np.ascontiguousarray(x[TOWN * c:TOWN * (c + 1)]),
            "wq": np.ascontiguousarray(Wqkv[:, qs]),
            "wk": np.ascontiguousarray(Wqkv[:, ks]),
            "wv": np.ascontiguousarray(Wqkv[:, vs]),
            "bq": np.ascontiguousarray(bqkv[qs].reshape(QC, 1)),
            "bkv": bkv,
            "bkd": bkd,
            "wout": Wout,
            "bout": bout,
            "wfcT_sh": np.ascontiguousarray(wfcT[(D // NC) * c:(D // NC) * (c + 1)]),
            "wprT_sh": np.ascontiguousarray(wprT[(FF // NC) * c:(FF // NC) * (c + 1)]),
        })
    return maps


def kernel(L0_attn_x, L0_attn_Wqkv, L0_attn_bqkv, L0_attn_Wout, L0_attn_bout,
           L0_ffn_w_fc, L0_ffn_w_proj):
    maps = _host_inputs(L0_attn_x, L0_attn_Wqkv, L0_attn_bqkv, L0_attn_Wout,
                        L0_attn_bout, L0_ffn_w_fc, L0_ffn_w_proj)
    nc, _ = build_program()
    res = run_bass_kernel_spmd(nc, maps, core_ids=list(range(NC)))
    out = np.concatenate([res.results[c]["out"] for c in range(NC)], axis=0)
    return np.ascontiguousarray(out.reshape(B, S, D))


# version 06
# speedup vs baseline: 1.1607x; 1.0940x over previous
import numpy as np
import ml_dtypes
from contextlib import ExitStack

import concourse.bass as bass
import concourse.mybir as mybir
from concourse.bass_utils import run_bass_kernel_spmd

F32 = mybir.dt.float32
BF16 = mybir.dt.bfloat16
AF = mybir.ActivationFunctionType
OP = mybir.AluOpType
AX = mybir.AxisListType
BF = ml_dtypes.bfloat16
ds = bass.ds

B, S, D = 2, 4096, 2048
T = B * S
NC = 8
TOWN = T // NC
HQ, HKV, HD, ROT = 32, 8, 64, 32
FF = 5120
NQH = HQ // NC
QC = NQH * HD
NWE = FF * D


class Ctr:
    def __init__(self, nc, name):
        self.h = nc.alloc_semaphore(name)
        self.n = 0

    def inc(self, instr, k=1):
        instr.then_inc(self.h, k)
        self.n += k
        return self.n


def _rope_tables():
    inv = 1.0 / (10000.0 ** (np.arange(0, ROT, 2, dtype=np.float64) / ROT))
    fr = np.outer(np.arange(S, dtype=np.float64), inv)
    cos = np.cos(fr).astype(np.float32)
    sin = np.sin(fr).astype(np.float32)
    Ct = np.ones((128, S), np.float32)
    St = np.zeros((128, S), np.float32)
    for base in (0, 64):
        Ct[base + 0:base + 16] = cos.T
        Ct[base + 16:base + 32] = cos.T
        St[base + 0:base + 16] = -sin.T
        St[base + 16:base + 32] = sin.T
    return Ct, St


def _perm128():
    Pm = np.zeros((128, 128), np.float32)
    for p in range(128):
        g, r = divmod(p, 32)
        Pm[32 * g + ((r + 16) % 32), p] = 1.0
    return Pm


def build_program(debug_taps=(), upto="full", no_stats=False):
    nc = bass.Bass("TRN2", target_bir_lowering=False, debug=False,
                   detect_race_conditions=False)

    x_own = nc.dram_tensor("x_own", [TOWN, D], F32, kind="ExternalInput").ap()
    wq_d = nc.dram_tensor("wq", [D, QC], F32, kind="ExternalInput").ap()
    wk_d = nc.dram_tensor("wk", [D, HD], F32, kind="ExternalInput").ap()
    wv_d = nc.dram_tensor("wv", [D, HD], F32, kind="ExternalInput").ap()
    bq_d = nc.dram_tensor("bq", [QC, 1], F32, kind="ExternalInput").ap()
    bkv_d = nc.dram_tensor("bkv", [128, 1], F32, kind="ExternalInput").ap()
    bkd_d = nc.dram_tensor("bkd", [128, 1], F32, kind="ExternalInput").ap()
    wout_d = nc.dram_tensor("wout", [D, D], F32, kind="ExternalInput").ap()
    bout_d = nc.dram_tensor("bout", [1, D], F32, kind="ExternalInput").ap()
    wfcT_d = nc.dram_tensor("wfcT_sh", [D // NC, FF], F32, kind="ExternalInput").ap()
    wprT_d = nc.dram_tensor("wprT_sh", [FF // NC, D], F32, kind="ExternalInput").ap()
    out_d = nc.dram_tensor("out", [TOWN, D], F32, kind="ExternalOutput").ap()

    xnag_i = nc.dram_tensor("xnag_i", [D, TOWN], BF16)
    xnag_o = nc.dram_tensor("xnag_o", [NC * D, TOWN], BF16, addr_space="Shared")
    stag_i = nc.dram_tensor("stag_i", [1, 2], F32)
    stag_o = nc.dram_tensor("stag_o", [1, 2], F32)
    wfag_i = nc.dram_tensor("wfag_i", [D // NC, FF], BF16)
    wfag_o = nc.dram_tensor("wfag_o", [D, FF], BF16, addr_space="Shared")
    wpag_i = nc.dram_tensor("wpag_i", [FF // NC, D], BF16)
    wpag_o = nc.dram_tensor("wpag_o", [FF, D], BF16, addr_space="Shared")
    atag_i = nc.dram_tensor("atag_i", [QC, T], BF16)
    barr_o = nc.dram_tensor("barr_o", [1, 2], F32)
    atag_o = nc.dram_tensor("atag_o", [D, T], BF16, addr_space="Shared")

    Cnp, Snp = _rope_tables()
    CT_d = nc.inline_tensor(Cnp, name="CTc")
    ST_d = nc.inline_tensor(Snp, name="STc")
    P_d = nc.inline_tensor(_perm128(), name="Pmc")
    tri_np = (np.arange(128)[None, :] >= np.arange(128)[:, None]).astype(BF)
    tri_d = nc.inline_tensor(tri_np, name="tric")
    ones_d = nc.inline_tensor(np.ones((128, 128), np.float32), name="onesc")
    id_d = nc.inline_tensor(np.eye(128, dtype=BF), name="idc")
    id64_np = np.zeros((128, 64), BF)
    id64_np[64:128] = np.eye(64, dtype=BF)
    id64_d = nc.inline_tensor(id64_np, name="id64c")

    taps = {}

    def tap(name, shape, dtype):
        if name in debug_taps:
            taps[name] = nc.dram_tensor("dbg_" + name, shape, dtype,
                                        kind="ExternalOutput")
        return taps.get(name)

    t_xnt = tap("xnt", [NC * D, TOWN], BF16)
    t_xno = tap("xno", [D, TOWN], BF16)
    t_tfc = tap("tfc", [D, FF], BF16)
    t_tpr = tap("tpr", [FF, D], BF16)
    t_qt = tap("qt", [QC, T], BF16)
    t_k2 = tap("k2", [128, T], BF16)
    t_vt = tap("vt", [128, 64 * 65], BF16)
    t_at = tap("at", [D, T], BF16)
    t_cb = tap("cb", [128, 8], F32)
    t_tfi = tap("tfi", [D // NC, FF], BF16)

    Cn = lambda name: Ctr(nc, name)
    sX, sSQ, sVA, sSD, sNM, sXN = (Cn("sX"), Cn("sSQ"), Cn("sVA"), Cn("sSD"),
                                   Cn("sNM"), Cn("sXN"))
    sTR, sXNT, sXC, sLDC, sWQ = (Cn("sTR"), Cn("sXNT"), Cn("sXC"), Cn("sLDC"),
                                 Cn("sWQ"))
    sTS, sSTv, sSTm, sSTe, sARd = (Cn("sTS"), Cn("sSTv"), Cn("sSTm"),
                                   Cn("sSTe"), Cn("sARd"))
    sCB, sSG, sTT, sTW = Cn("sCB"), Cn("sSG"), Cn("sTT"), Cn("sTW")
    sARc, sAGXc, sAGWc, sAGAc = (Cn("sARc"), Cn("sAGXc"), Cn("sAGWc"),
                                 Cn("sAGAc"))
    sRH, sCS, sQP, sQE, sQS = Cn("sRH"), Cn("sCS"), Cn("sQP"), Cn("sQE"), Cn("sQS")
    sT1, sT2, sRP, sTC, sTV, sVC = (Cn("sT1"), Cn("sT2"), Cn("sRP"), Cn("sTC"),
                                    Cn("sTV"), Cn("sVC"))
    sSC, sEXP, sTRI, sAV, sRC, sBC, sEPI, sSTG = (
        Cn("sSC"), Cn("sEXP"), Cn("sTRI"), Cn("sAV"), Cn("sRC"), Cn("sBC"),
        Cn("sEPI"), Cn("sSTG"))
    sFS, sFMM, sFE, sFH = Cn("sFS"), Cn("sFMM"), Cn("sFE"), Cn("sFH")
    sPC, sWC, sXR, sOP, sOE, sOS = (Cn("sPC"), Cn("sWC"), Cn("sXR"), Cn("sOP"),
                                    Cn("sOE"), Cn("sOS"))
    sDBG = Cn("sDBG")
    sTS2 = Cn("sTS2")
    sFNC = Cn("sFNC")

    def vf(v, ins):
        sFNC.inc(ins, 1)
        v.wait_ge(sFNC.h, sFNC.n)

    est = ExitStack()
    with est:
        xnT = est.enter_context(nc.sbuf_tensor("xnT", [128, 16 * TOWN], BF16))
        Pf = est.enter_context(nc.sbuf_tensor("Pf", [128, 128], F32))
        trib = est.enter_context(nc.sbuf_tensor("trib", [128, 128], BF16))
        onesf = est.enter_context(nc.sbuf_tensor("onesf", [128, 128], F32))
        idb = est.enter_context(nc.sbuf_tensor("idb", [128, 128], BF16))
        id64 = est.enter_context(nc.sbuf_tensor("id64", [128, 64], BF16))
        bq_sb = est.enter_context(nc.sbuf_tensor("bq_sb", [128, 2], F32))
        bkv_sb = est.enter_context(nc.sbuf_tensor("bkv_sb", [128, 1], F32))
        bkd_sb = est.enter_context(nc.sbuf_tensor("bkd_sb", [128, 1], F32))
        cb4 = est.enter_context(nc.sbuf_tensor("cb4", [128, 4], F32))
        stat4 = est.enter_context(nc.sbuf_tensor("stat4", [128, 4], F32))
        st2 = est.enter_context(nc.sbuf_tensor("st2", [1, 4], F32))
        sln = est.enter_context(nc.sbuf_tensor("sln", [128, 16], F32))
        qstage = [est.enter_context(nc.sbuf_tensor(f"qstage{i}", [128, 512], F32))
                  for i in range(2)]
        tmp1 = est.enter_context(nc.sbuf_tensor("tmp1", [128, 512], F32))
        vstage = est.enter_context(nc.sbuf_tensor("vstage", [128, 512], BF16))
        pstage = [est.enter_context(nc.sbuf_tensor(f"pstage{i}", [128, 2, 512], BF16))
                  for i in range(2)]
        stg = [est.enter_context(nc.sbuf_tensor(f"stg{i}", [64, 512], BF16))
               for i in range(4)]
        rs_sb = est.enter_context(nc.sbuf_tensor("rs_sb", [128, 2, 512], F32))
        rb_sb = est.enter_context(nc.sbuf_tensor("rb_sb", [128, 2, 512], F32))

        block = est.enter_context(nc.Block())

        @block.gpsimd
        def _(g):
            for dst, src in ((Pf[:, :], P_d.ap()), (trib[:, :], tri_d.ap()),
                             (onesf[:, :], ones_d.ap()), (idb[:, :], id_d.ap()),
                             (id64[:, :], id64_d.ap()),
                             (bq_sb[:, 0:1], bq_d[0:128, :]),
                             (bq_sb[:, 1:2], bq_d[128:256, :]),
                             (bkv_sb[:, :], bkv_d[:, :]),
                             (bkd_sb[:, :], bkd_d[:, :])):
                sLDC.inc(g.dma_start(out=dst, in_=src), 16)

        qkv_scope = ExitStack()
        with qkv_scope:
            QT = [qkv_scope.enter_context(nc.sbuf_tensor(f"QT{i}", [128, T], BF16))
                  for i in range(2)]
            K2 = qkv_scope.enter_context(nc.sbuf_tensor("K2", [128, T], BF16))
            Vt = qkv_scope.enter_context(nc.sbuf_tensor("Vt", [128, 64, 65], BF16))

            r1 = ExitStack()
            with r1:
                x_db = [r1.enter_context(nc.sbuf_tensor(f"x_db{i}", [128, D], F32))
                        for i in range(2)]
                xn_db = [r1.enter_context(nc.sbuf_tensor(f"xn_db{i}", [128, D], BF16))
                         for i in range(2)]
                sqdump = r1.enter_context(nc.sbuf_tensor("sqdump", [128, D], BF16))
                tsrc = [r1.enter_context(nc.sbuf_tensor(f"tsrc{i}", [128, 512], F32))
                        for i in range(2)]
                t1buf = r1.enter_context(nc.sbuf_tensor("t1buf", [128, 512], F32))
                t2buf = r1.enter_context(nc.sbuf_tensor("t2buf", [128, 512], F32))
                tstg = [r1.enter_context(nc.sbuf_tensor(f"tstg{i}", [128, 512], BF16))
                        for i in range(2)]
                csb = [r1.enter_context(nc.sbuf_tensor(f"csb{i}", [128, 2, 512], F32))
                       for i in range(2)]
                rhs_db = [r1.enter_context(
                    nc.sbuf_tensor(f"rhs{i}", [128, 16, 512], BF16)) for i in range(2)]
                wq_sb = r1.enter_context(nc.sbuf_tensor("wq_sb", [128, 16 * QC], BF16))
                wkv_sb = r1.enter_context(nc.sbuf_tensor("wkv_sb", [128, 16 * 128], BF16))

                pA = ExitStack()
                ptA = [pA.enter_context(nc.psum_tensor(f"ptA{i}", [128, 4, 128], BF16))
                       for i in range(2)]
                pmisc = pA.enter_context(nc.psum_tensor("pmisc", [128, 512], F32))

                @block.sync
                def _(sy):
                    for ti in range(8):
                        if ti >= 2:
                            sy.wait_ge(sXN.h, ti - 1)
                        sX.inc(sy.dma_start(
                            out=x_db[ti % 2][:, :],
                            in_=x_own[128 * ti:128 * (ti + 1), :]), 16)

                @block.scalar
                def _(sc):
                    for ti in range(9):
                        if ti < 8:
                            sc.wait_ge(sX.h, 16 * (ti + 1))
                            c = 8 * (ti % 2)
                            sSQ.inc(sc.activation(
                                out=sqdump[:, :], in_=x_db[ti % 2][:, :],
                                func=AF.Square, accum_out=sln[:, c + 4:c + 5]), 1)
                        if ti >= 1:
                            tj = ti - 1
                            c = 8 * (tj % 2)
                            sc.wait_ge(sVA.h, tj + 1)
                            sSD.inc(sc.activation(
                                out=sln[:, c + 6:c + 7], in_=sln[:, c + 5:c + 6],
                                func=AF.Sqrt), 1)
                            sc.wait_ge(sNM.h, tj + 1)
                            if tj >= 2:
                                sc.wait_ge(sTR.h, 4 * tj - 4)
                            sXN.inc(sc.activation(
                                out=xn_db[tj % 2][:, :], in_=x_db[tj % 2][:, :],
                                func=AF.Identity, bias=sln[:, c + 7:c + 8],
                                scale=sln[:, c + 6:c + 7]), 1)

                @block.vector
                def _(v):
                    inv_d = 1.0 / D
                    for ti in range(10):
                        if ti < 8:
                            c = 8 * (ti % 2)
                            v.wait_ge(sX.h, 16 * (ti + 1))
                            v.wait_ge(sSQ.h, ti + 1)
                            if ti >= 2:
                                v.wait_ge(sXN.h, ti - 1)
                            vf(v, v.reduce_sum(out=sln[:, c + 0:c + 1],
                                                in_=x_db[ti % 2][:, :], axis=AX.X))
                            vf(v, v.tensor_scalar(out=sln[:, c + 1:c + 2],
                                                  in0=sln[:, c + 0:c + 1],
                                                  scalar1=inv_d, scalar2=None,
                                                  op0=OP.mult))
                            vf(v, v.tensor_tensor(out=sln[:, c + 2:c + 3],
                                                  in0=sln[:, c + 1:c + 2],
                                                  in1=sln[:, c + 1:c + 2],
                                                  op=OP.mult))
                            vf(v, v.tensor_scalar(out=sln[:, c + 3:c + 4],
                                                  in0=sln[:, c + 4:c + 5],
                                                  scalar1=inv_d, scalar2=None,
                                                  op0=OP.mult))
                            vf(v, v.tensor_tensor(out=sln[:, c + 3:c + 4],
                                                  in0=sln[:, c + 3:c + 4],
                                                  in1=sln[:, c + 2:c + 3],
                                                  op=OP.subtract))
                            vf(v, v.tensor_scalar(out=sln[:, c + 3:c + 4],
                                                  in0=sln[:, c + 3:c + 4],
                                                  scalar1=1e-5, scalar2=None,
                                                  op0=OP.add))
                            sVA.inc(v.reciprocal(out=sln[:, c + 5:c + 6],
                                                 in_=sln[:, c + 3:c + 4]), 1)
                        if ti >= 1 and ti <= 8:
                            tj = ti - 1
                            c = 8 * (tj % 2)
                            v.wait_ge(sSD.h, tj + 1)
                            sNM.inc(v.tensor_scalar(
                                out=sln[:, c + 7:c + 8], in0=sln[:, c + 1:c + 2],
                                scalar1=sln[:, c + 6:c + 7], scalar2=-1.0,
                                op0=OP.mult, op1=OP.mult), 1)
                        if ti >= 2:
                            te2 = ti - 2
                            for gj in range(4):
                                G = 4 * te2 + gj
                                v.wait_ge(sTR.h, G + 1)
                                dst = xnT.ap().rearrange(
                                    "p (dj t) -> p dj t", dj=16)[
                                    :, 4 * gj:4 * gj + 4,
                                    128 * te2:128 * (te2 + 1)]
                                sXNT.inc(v.tensor_copy(out=dst,
                                                       in_=ptA[G % 2][:, :, :]), 1)
                    v.memset(Vt[:, :, 64:65], 1.0)

                @block.tensor
                def _(t):
                    t.wait_ge(sLDC.h, sLDC.n)
                    for ti in range(8):
                        t.wait_ge(sXN.h, ti + 1)
                        for gj in range(4):
                            G = 4 * ti + gj
                            if G >= 2:
                                t.wait_ge(sXNT.h, G - 1)
                            for i in range(4):
                                dj = 4 * gj + i
                                ins = t.transpose(
                                    ptA[G % 2][:, i, :],
                                    xn_db[ti % 2][:, 128 * dj:128 * (dj + 1)],
                                    idb[:, :])
                            sTR.inc(ins, 1)

                @block.sync
                def _(sy):
                    sy.wait_ge(sXNT.h, 32)
                    for dj in range(16):
                        sXC.inc(sy.dma_start(
                            out=xnag_i.ap()[128 * dj:128 * (dj + 1), :],
                            in_=xnT[:, TOWN * dj:TOWN * (dj + 1)]), 16)

                if no_stats:
                    @block.gpsimd
                    def _(g):
                        g.wait_ge(sXC.h, 256)
                        sAGXc.inc(g.collective_compute(
                            "AllGather", OP.bypass,
                            replica_groups=[list(range(NC))],
                            ins=[xnag_i.ap()], outs=[xnag_o.ap()]), 1)

                if not no_stats:
                    pieces = []
                    for rr in range(2):
                        for cc in range(10):
                            r0, c0 = 128 * rr, 512 * cc
                            pieces.append((wfcT_d[r0:r0 + 128, c0:c0 + 512], 512, 0,
                                           wfag_i.ap()[r0:r0 + 128, c0:c0 + 512]))
                    for rr in range(5):
                        for cc in range(4):
                            r0, c0 = 128 * rr, 512 * cc
                            pieces.append((wprT_d[r0:r0 + 128, c0:c0 + 512], 512, 1,
                                           wpag_i.ap()[r0:r0 + 128, c0:c0 + 512]))
                    NP = len(pieces)

                    @block.gpsimd
                    def _(g):
                        for p, (src, w, mat, dst) in enumerate(pieces):
                            if p >= 2:
                                g.wait_ge(sSTv.h, p - 1)
                            sTS.inc(g.dma_start(out=tsrc[p % 2][:, 0:w], in_=src), 16)
                        for a in range(16):
                            sWQ.inc(g.dma_start(
                                out=wq_sb[:, QC * a:QC * (a + 1)],
                                in_=wq_d[128 * a:128 * (a + 1), :]), 16)
                            sWQ.inc(g.dma_start(
                                out=wkv_sb[:, 128 * a:128 * a + 64],
                                in_=wk_d[128 * a:128 * (a + 1), :]), 16)
                            sWQ.inc(g.dma_start(
                                out=wkv_sb[:, 128 * a + 64:128 * a + 128],
                                in_=wv_d[128 * a:128 * (a + 1), :]), 16)

                    @block.vector
                    def _(v):
                        for p, (src, w, mat, dst) in enumerate(pieces):
                            v.wait_ge(sTS.h, 16 * (p + 1))
                            first = (p == 0) or (pieces[p][2] != pieces[p - 1][2])
                            col = mat
                            if first:
                                sSTv.inc(v.reduce_sum(
                                    out=stat4[:, col:col + 1],
                                    in_=tsrc[p % 2][:, 0:w], axis=AX.X,
                                    apply_absolute_value=True), 1)
                                v.wait_ge(sSTv.h, sSTv.n)
                            else:
                                vf(v, v.reduce_sum(
                                    out=stat4[:, 2:3],
                                    in_=tsrc[p % 2][:, 0:w], axis=AX.X,
                                    apply_absolute_value=True))
                                ins = v.tensor_tensor(
                                    out=stat4[:, col:col + 1],
                                    in0=stat4[:, col:col + 1],
                                    in1=stat4[:, 2:3], op=OP.add)
                                sSTv.inc(ins, 1)
                                v.wait_ge(sSTv.h, sSTv.n)

                    @block.tensor
                    def _(t):
                        t.wait_ge(sSTv.h, NP)
                        sSTm.inc(t.matmul(pmisc[0:1, 0:2], onesf[:, 0:1],
                                          stat4[:, 0:2], start=True, stop=True), 1)

                    @block.vector
                    def _(v):
                        v.wait_ge(sSTm.h, 1)
                        sSTe.inc(v.tensor_copy(out=st2[0:1, 0:2],
                                               in_=pmisc[0:1, 0:2]), 1)

                    @block.gpsimd
                    def _(g):
                        g.wait_ge(sSTe.h, 1)
                        sARd.inc(g.dma_start(out=stag_i.ap(), in_=st2[0:1, 0:2]), 16)
                        g.wait_ge(sARd.h, 16)
                        sARc.inc(g.collective_compute(
                            "AllReduce", OP.add, replica_groups=[list(range(NC))],
                            ins=[stag_i.ap()], outs=[stag_o.ap()]), 1)
                        g.wait_ge(sARc.h, 1)
                        sARd.inc(g.dma_start(out=st2[0:1, 2:4], in_=stag_o.ap()), 16)
                        g.wait_ge(sARd.h, 32)
                        g.wait_ge(sXC.h, 256)
                        sAGXc.inc(g.collective_compute(
                            "AllGather", OP.bypass, replica_groups=[list(range(NC))],
                            ins=[xnag_i.ap()], outs=[xnag_o.ap()]), 1)

                    @block.vector
                    def _(v):
                        v.wait_ge(sARd.h, 32)
                        vf(v, v.tensor_scalar(out=st2[0:1, 0:2], in0=st2[0:1, 2:4],
                                              scalar1=1.0 / NWE, scalar2=1e-5,
                                              op0=OP.mult, op1=OP.max))
                        sSTe.inc(v.tensor_scalar(out=stat4[0:1, 0:2],
                                                 in0=st2[0:1, 0:2],
                                                 scalar1=0.5, scalar2=None,
                                                 op0=OP.mult), 1)
                        sSTe.inc(v.tensor_scalar(out=stat4[0:1, 2:4],
                                                 in0=st2[0:1, 0:2],
                                                 scalar1=-0.7, scalar2=None,
                                                 op0=OP.mult), 1)

                    @block.tensor
                    def _(t):
                        t.wait_ge(sSTe.h, 3)
                        sSTm.inc(t.matmul(pmisc[:, 0:4], onesf[0:1, :],
                                          stat4[0:1, 0:4], start=True, stop=True), 1)

                    @block.vector
                    def _(v):
                        v.wait_ge(sSTm.h, 2)
                        sCB.inc(v.tensor_copy(out=cb4[:, :], in_=pmisc[:, 0:4]), 1)

                    @block.gpsimd
                    def _(g):
                        g.wait_ge(sTW.h, 16 * NP)
                        g.wait_ge(sAGXc.h, 1)
                        sAGWc.inc(g.collective_compute(
                            "AllGather", OP.bypass, replica_groups=[list(range(NC))],
                            ins=[wfag_i.ap()], outs=[wfag_o.ap()]), 1)
                        g.wait_ge(sAGWc.h, 1)
                        sAGWc.inc(g.collective_compute(
                            "AllGather", OP.bypass, replica_groups=[list(range(NC))],
                            ins=[wpag_i.ap()], outs=[wpag_o.ap()]), 1)

                    @block.scalar
                    def _(sc):
                        sc.wait_ge(sCB.h, 1)
                        for pp in (0, 1):
                            if pp == 0:
                                sc.wait_ge(sSTv.h, NP)
                                sc.wait_ge(sAGXc.h, 1)
                                sc.wait_ge(sARd.h, 32)
                            src2, w2b, _, _ = pieces[pp]
                            sTS2.inc(sc.dma_start(out=tsrc[pp % 2][:, 0:w2b],
                                                  in_=src2), 16)
                        for p, (src, w, mat, dst) in enumerate(pieces):
                            sc.wait_ge(sTS2.h, 16 * (p + 1))
                            if p >= 1:
                                sc.wait_ge(sTT.h, 3 * p)
                            bias = cb4[:, 2 + mat:3 + mat]
                            sSG.inc(sc.activation(out=t1buf[:, 0:w],
                                                  in_=tsrc[p % 2][:, 0:w],
                                                  func=AF.Sign, bias=bias), 1)
                            sSG.inc(sc.activation(out=t2buf[:, 0:w],
                                                  in_=tsrc[p % 2][:, 0:w],
                                                  func=AF.Sign, bias=bias,
                                                  scale=-1.0), 1)
                            if p + 2 < NP:
                                sc.wait_ge(sSG.h, sSG.n)
                                src2, w2b, _, _ = pieces[p + 2]
                                sTS2.inc(sc.dma_start(
                                    out=tsrc[(p + 2) % 2][:, 0:w2b], in_=src2), 16)
                            if p >= 2:
                                p2 = p - 2
                                _, w2b, _, dst2 = pieces[p2]
                                sc.wait_ge(sTT.h, 3 * (p2 + 1))
                                sTW.inc(sc.dma_start(out=dst2,
                                                     in_=tstg[p2 % 2][:, 0:w2b]), 16)
                        for p2 in (NP - 2, NP - 1):
                            _, w2b, _, dst2 = pieces[p2]
                            sc.wait_ge(sTT.h, 3 * (p2 + 1))
                            sTW.inc(sc.dma_start(out=dst2,
                                                 in_=tstg[p2 % 2][:, 0:w2b]), 16)

                    @block.vector
                    def _(v):
                        for p, (src, w, mat, dst) in enumerate(pieces):
                            v.wait_ge(sSG.h, 2 * (p + 1))
                            if p >= 2:
                                v.wait_ge(sTW.h, 16 * (p - 1))
                            ins = v.tensor_tensor(out=t1buf[:, 0:w],
                                                  in0=t1buf[:, 0:w],
                                                  in1=t2buf[:, 0:w],
                                                  op=OP.subtract)
                            sTT.inc(ins, 1)
                            v.wait_ge(sTT.h, sTT.n)
                            sTT.inc(v.tensor_scalar(out=tstg[p % 2][:, 0:w],
                                                    in0=t1buf[:, 0:w],
                                                    scalar1=cb4[:, mat:mat + 1],
                                                    scalar2=None, op0=OP.mult), 2)


                if (t_xnt is not None or t_tfc is not None or t_xno is not None
                        or t_cb is not None or t_tfi is not None):
                    @block.sync
                    def _(sy):
                        if t_xno is not None:
                            sy.wait_ge(sXNT.h, 32)
                            for dj in range(16):
                                sDBG.inc(sy.dma_start(
                                    out=t_xno.ap()[128 * dj:128 * (dj + 1), :],
                                    in_=xnT[:, TOWN * dj:TOWN * (dj + 1)]), 16)
                        if t_xnt is not None:
                            sy.wait_ge(sAGXc.h, 1)
                            sDBG.inc(sy.dma_start(out=t_xnt.ap(), in_=xnag_o.ap()), 16)
                        if t_cb is not None:
                            sy.wait_ge(sCB.h, 1)
                            sDBG.inc(sy.dma_start(out=t_cb.ap()[:, 0:4],
                                                  in_=cb4[:, :]), 16)
                            sDBG.inc(sy.dma_start(out=t_cb.ap()[0:1, 4:8],
                                                  in_=st2[:, :]), 16)
                        if t_tfi is not None:
                            sy.wait_ge(sTW.h, 16 * NP)
                            sDBG.inc(sy.dma_start(out=t_tfi.ap(), in_=wfag_i.ap()), 16)
                        if t_tfc is not None:
                            sy.wait_ge(sAGWc.h, 2)
                            sDBG.inc(sy.dma_start(out=t_tfc.ap(), in_=wfag_o.ap()), 16)
                            sDBG.inc(sy.dma_start(out=t_tpr.ap(), in_=wpag_o.ap()), 16)

                if upto == "A":
                    _drain(nc, block, out_d, [c for c in (sDBG,) if c.n], [sOS])
                    return nc, taps

                pA.close()

                pC = ExitStack()
                pacc = [pC.enter_context(nc.psum_tensor(f"pacc{i}", [128, 512], F32))
                        for i in range(2)]
                pqs = [pC.enter_context(nc.psum_tensor(f"pqs{i}", [128, 512], F32))
                       for i in range(2)]
                ptv = pC.enter_context(nc.psum_tensor("ptv", [128, 4, 64], BF16))

                qe_of = lambda ts, k: 5 * ts + k
                qp_of = lambda ts, k: 4 * ts + k
                qs_of = lambda ts, k: 3 * ts + k
                t1_of = lambda ts, k: 3 * ts + k
                t2_of = lambda ts, k: 3 * ts + k
                rp_of = lambda ts, k: 3 * ts + k

                @block.sync
                def _(sy):
                    sy.wait_ge(sAGXc.h, 1)
                    for ts in range(16):
                        if ts >= 2:
                            sy.wait_ge(sQP.h, 4 * (ts - 2) + 4)
                        r, cpart = ts // 2, ts % 2
                        sRH.inc(sy.dma_start(
                            out=rhs_db[ts % 2][:, :, :],
                            in_=xnag_o.ap()[2048 * r:2048 * (r + 1),
                                            512 * cpart:512 * (cpart + 1)].rearrange(
                                "(a p) f -> p a f", p=128)), 16)
                        s0 = 512 * (ts % 8)
                        if ts >= 2:
                            sy.wait_ge(sRP.h, rp_of(ts - 2, 3))
                        sCS.inc(sy.dma_start(out=csb[ts % 2][:, 0, :],
                                             in_=CT_d.ap()[:, s0:s0 + 512]), 16)
                        sCS.inc(sy.dma_start(out=csb[ts % 2][:, 1, :],
                                             in_=ST_d.ap()[:, s0:s0 + 512]), 16)

                @block.tensor
                def _(t):
                    t.wait_ge(sWQ.h, sWQ.n)
                    t.wait_ge(sCB.h, 1)
                    t.wait_ge(sVC.h, 0)
                    for ts in range(16):
                        t.wait_ge(sRH.h, 16 * (ts + 1))
                        if ts >= 1:
                            t.wait_ge(sQE.h, qe_of(ts - 1, 4))
                        for d in range(16):
                            ins = t.matmul(pacc[0][:, :],
                                           wq_sb[:, QC * d:QC * d + 128],
                                           rhs_db[ts % 2][:, d, :],
                                           start=(d == 0), stop=(d == 15))
                        sQP.inc(ins, 1)
                        if ts >= 1:
                            t.wait_ge(sQE.h, qe_of(ts - 1, 5))
                        for d in range(16):
                            ins = t.matmul(pacc[1][:, :],
                                           wq_sb[:, QC * d + 128:QC * (d + 1)],
                                           rhs_db[ts % 2][:, d, :],
                                           start=(d == 0), stop=(d == 15))
                        sQP.inc(ins, 1)
                        t.wait_ge(sQE.h, qe_of(ts, 1))
                        if ts >= 1:
                            t.wait_ge(sT2.h, t2_of(ts - 1, 3))
                        sQS.inc(t.matmul(pqs[0][:, :], Pf[:, :], qstage[0][:, :],
                                         start=True, stop=True), 1)
                        t.wait_ge(sQE.h, qe_of(ts, 1))
                        for d in range(16):
                            ins = t.matmul(pacc[0][:, :],
                                           wkv_sb[:, 128 * d:128 * (d + 1)],
                                           rhs_db[ts % 2][:, d, :],
                                           start=(d == 0), stop=(d == 15))
                        sQP.inc(ins, 1)
                        t.wait_ge(sQE.h, qe_of(ts, 2))
                        if ts >= 1:
                            t.wait_ge(sT2.h, t2_of(ts - 1, 2))
                        sQS.inc(t.matmul(pqs[1][:, :], Pf[:, :], qstage[1][:, :],
                                         start=True, stop=True), 1)
                        t.wait_ge(sQE.h, qe_of(ts, 2))
                        for d in range(16):
                            ins = t.matmul(pacc[1][64:128, :],
                                           wkv_sb[:, 128 * d:128 * d + 64],
                                           rhs_db[ts % 2][:, d, :],
                                           start=(d == 0), stop=(d == 15))
                        sQP.inc(ins, 1)
                        t.wait_ge(sQE.h, qe_of(ts, 4))
                        if ts >= 1:
                            t.wait_ge(sVC.h, ts)
                        for i in range(4):
                            ins = t.transpose(ptv[:, i, :],
                                              vstage[64:128, 128 * i:128 * (i + 1)],
                                              id64[64:128, :])
                        sTV.inc(ins, 1)
                        t.wait_ge(sQE.h, qe_of(ts, 5))
                        t.wait_ge(sT2.h, t2_of(ts, 1))
                        sQS.inc(t.matmul(pqs[0][:, :], Pf[:, :], qstage[0][:, :],
                                         start=True, stop=True), 1)

                @block.scalar
                def _(sc):
                    for ts in range(16):
                        sc.wait_ge(sQP.h, qp_of(ts, 1))
                        if ts >= 1:
                            sc.wait_ge(sT1.h, t1_of(ts - 1, 3))
                            sc.wait_ge(sQS.h, qs_of(ts - 1, 3))
                            sc.wait_ge(sRP.h, rp_of(ts - 1, 3))
                        sQE.inc(sc.activation(out=qstage[0][:, :], in_=pacc[0][:, :],
                                              func=AF.Identity,
                                              bias=bq_sb[:, 0:1]), 1)
                        sc.wait_ge(sQP.h, qp_of(ts, 2))
                        if ts >= 1:
                            sc.wait_ge(sT1.h, t1_of(ts - 1, 2))
                            sc.wait_ge(sQS.h, qs_of(ts - 1, 2))
                            sc.wait_ge(sRP.h, rp_of(ts - 1, 2))
                        sQE.inc(sc.activation(out=qstage[1][:, :], in_=pacc[1][:, :],
                                              func=AF.Identity,
                                              bias=bq_sb[:, 1:2]), 1)
                        sc.wait_ge(sQP.h, qp_of(ts, 3))
                        sc.wait_ge(sT1.h, t1_of(ts, 1))
                        sc.wait_ge(sQS.h, qs_of(ts, 1))
                        sc.wait_ge(sRP.h, rp_of(ts, 1))
                        sQE.inc(sc.activation(out=qstage[0][0:64, :],
                                              in_=pacc[0][0:64, :],
                                              func=AF.Identity,
                                              bias=bkv_sb[0:64, :]), 1)
                        if ts >= 1:
                            sc.wait_ge(sTV.h, ts)
                        sQE.inc(sc.activation(out=vstage[64:128, :],
                                              in_=pacc[0][64:128, :],
                                              func=AF.Identity,
                                              bias=bkv_sb[64:128, :]), 1)
                        sc.wait_ge(sQP.h, qp_of(ts, 4))
                        sQE.inc(sc.activation(out=qstage[0][64:128, :],
                                              in_=pacc[1][64:128, :],
                                              func=AF.Identity,
                                              bias=bkd_sb[64:128, :]), 1)

                @block.vector
                def _(v):
                    for ts in range(16):
                        col = slice(512 * ts, 512 * (ts + 1))
                        v.wait_ge(sCS.h, 32 * (ts + 1))
                        v.wait_ge(sQE.h, qe_of(ts, 1))
                        sT1.inc(v.tensor_tensor(out=tmp1[:, :],
                                                in0=qstage[0][:, :],
                                                in1=csb[ts % 2][:, 0, :],
                                                op=OP.mult), 1)
                        v.wait_ge(sT1.h, sT1.n)
                        v.wait_ge(sQS.h, qs_of(ts, 1))
                        sT2.inc(v.tensor_tensor(out=qstage[0][:, :],
                                                in0=csb[ts % 2][:, 1, :],
                                                in1=pqs[0][:, :], op=OP.mult), 1)
                        v.wait_ge(sT2.h, sT2.n)
                        sRP.inc(v.tensor_tensor(out=QT[0][:, col], in0=tmp1[:, :],
                                                in1=qstage[0][:, :], op=OP.add), 1)
                        v.wait_ge(sQE.h, qe_of(ts, 2))
                        sT1.inc(v.tensor_tensor(out=tmp1[:, :],
                                                in0=qstage[1][:, :],
                                                in1=csb[ts % 2][:, 0, :],
                                                op=OP.mult), 1)
                        v.wait_ge(sT1.h, sT1.n)
                        v.wait_ge(sQS.h, qs_of(ts, 2))
                        sT2.inc(v.tensor_tensor(out=qstage[1][:, :],
                                                in0=csb[ts % 2][:, 1, :],
                                                in1=pqs[1][:, :], op=OP.mult), 1)
                        v.wait_ge(sT2.h, sT2.n)
                        sRP.inc(v.tensor_tensor(out=QT[1][:, col], in0=tmp1[:, :],
                                                in1=qstage[1][:, :], op=OP.add), 1)
                        v.wait_ge(sQE.h, qe_of(ts, 5))
                        sT1.inc(v.tensor_tensor(out=tmp1[:, :],
                                                in0=qstage[0][:, :],
                                                in1=csb[ts % 2][:, 0, :],
                                                op=OP.mult), 1)
                        v.wait_ge(sT1.h, sT1.n)
                        v.wait_ge(sQS.h, qs_of(ts, 3))
                        sT2.inc(v.tensor_tensor(out=qstage[0][:, :],
                                                in0=csb[ts % 2][:, 1, :],
                                                in1=pqs[0][:, :], op=OP.mult), 1)
                        v.wait_ge(sT2.h, sT2.n)
                        sRP.inc(v.tensor_tensor(out=K2[:, col], in0=tmp1[:, :],
                                                in1=qstage[0][:, :], op=OP.add), 1)
                        v.wait_ge(sTV.h, ts + 1)
                        sVC.inc(v.tensor_copy(out=Vt[:, 4 * ts:4 * ts + 4, 0:64],
                                              in_=ptv[:, :, :]), 1)

                if t_qt is not None:
                    @block.sync
                    def _(sy):
                        sy.wait_ge(sRP.h, 48)
                        sy.wait_ge(sVC.h, 16)
                        sDBG.inc(sy.dma_start(out=t_qt.ap()[0:128, :],
                                              in_=QT[0][:, :]), 16)
                        sDBG.inc(sy.dma_start(out=t_qt.ap()[128:256, :],
                                              in_=QT[1][:, :]), 16)
                        sDBG.inc(sy.dma_start(out=t_k2.ap(), in_=K2[:, :]), 16)
                        sDBG.inc(sy.dma_start(
                            out=t_vt.ap(),
                            in_=Vt.ap().rearrange("p a b -> p (a b)")), 16)

                if upto == "C":
                    _drain(nc, block, out_d, [sDBG], [sOS])
                    return nc, taps

                pC.close()
            r1.close()

            pD = ExitStack()
            psct = [pD.enter_context(nc.psum_tensor(f"psct{i}", [128, 2, 512], F32))
                    for i in range(2)]
            pav = [pD.enter_context(nc.psum_tensor(f"pav{i}", [128, 512], F32))
                   for i in range(2)]
            pbc = pD.enter_context(nc.psum_tensor("pbc", [128, 2, 512], F32))

            iters = []
            qidx = 0
            for b in range(2):
                for hp in range(2):
                    for qtc in range(8):
                        njs = 4 * qtc + 4
                        for j in range(njs):
                            iters.append((b, hp, qtc, j, njs, qidx))
                        qidx += 1
            NIT = len(iters)
            tri_pref = []
            c = 0
            for (b, hp, qtc, j, njs, qx) in iters:
                if j >= 4 * qtc:
                    c += 1
                tri_pref.append(c)

            def _av(t, it, i):
                b, hp, qtc, j, njs, qx = it
                qt0 = 4096 * b + 512 * qtc
                f0 = 128 * (j - 4 * qtc) if j >= 4 * qtc else 0
                kb = 32 * b + j
                t.wait_ge(sEXP.h, i + 1)
                if j >= 4 * qtc:
                    t.wait_ge(sTRI.h, 2 * tri_pref[i])
                if j == 0 and qx >= 1:
                    t.wait_ge(sEPI.h, 2 * qx)
                pv = 0
                t.matmul(pav[pv][0:65, f0:512], Vt[:, kb, :],
                         pstage[i % 2][:, 0, f0:512],
                         start=(j == 0), stop=(j == njs - 1))
                sAV.inc(t.matmul(pav[pv + 1][0:65, f0:512], Vt[:, kb, :],
                                 pstage[i % 2][:, 1, f0:512],
                                 start=(j == 0), stop=(j == njs - 1)), 1)

            @block.tensor
            def _(t):
                t.wait_ge(sRP.h, 48)
                t.wait_ge(sVC.h, 16)
                t.wait_ge(sQE.h, 80)
                last_of_q = {}
                for i, (b, hp, qtc, j, njs, qx) in enumerate(iters):
                    if j == njs - 1:
                        last_of_q[qx] = i
                for i, (b, hp, qtc, j, njs, qx) in enumerate(iters):
                    qt0 = 4096 * b + 512 * qtc
                    kt0 = 4096 * b + 128 * j
                    f0 = 128 * (j - 4 * qtc) if j >= 4 * qtc else 0
                    if i >= 2:
                        t.wait_ge(sEXP.h, i - 1)
                    t.matmul(psct[i % 2][:, 0, f0:512],
                             K2[0:64, kt0:kt0 + 128],
                             QT[hp][0:64, qt0 + f0:qt0 + 512],
                             start=True, stop=True)
                    sSC.inc(t.matmul(psct[i % 2][:, 1, f0:512],
                                     K2[64:128, kt0:kt0 + 128],
                                     QT[hp][64:128, qt0 + f0:qt0 + 512],
                                     start=True, stop=True), 1)
                    if i >= 1:
                        _av(t, iters[i - 1], i - 1)
                    for qx2, li in list(last_of_q.items()):
                        if li == i - 1:
                            t.wait_ge(sRC.h, 2 * qx2 + 2)
                            if qx2 >= 1:
                                t.wait_ge(sEPI.h, 2 * qx2)
                            t.matmul(pbc[0:64, 0, :], onesf[64:65, 0:64],
                                     rs_sb[64:65, 0, :], start=True, stop=True)
                            sBC.inc(t.matmul(pbc[0:64, 1, :], onesf[64:65, 0:64],
                                             rs_sb[64:65, 1, :],
                                             start=True, stop=True), 2)
                            del last_of_q[qx2]
                _av(t, iters[NIT - 1], NIT - 1)
                qx2 = iters[NIT - 1][5]
                t.wait_ge(sRC.h, 2 * qx2 + 2)
                t.wait_ge(sEPI.h, 2 * qx2)
                t.matmul(pbc[0:64, 0, :], onesf[64:65, 0:64],
                         rs_sb[64:65, 0, :], start=True, stop=True)
                sBC.inc(t.matmul(pbc[0:64, 1, :], onesf[64:65, 0:64],
                                 rs_sb[64:65, 1, :], start=True, stop=True), 2)

            @block.scalar
            def _(sc):
                for i, (b, hp, qtc, j, njs, qx) in enumerate(iters):
                    f0 = 128 * (j - 4 * qtc) if j >= 4 * qtc else 0
                    sc.wait_ge(sSC.h, i + 1)
                    if i >= 2:
                        sc.wait_ge(sAV.h, i - 1)
                    sEXP.inc(sc.activation(out=pstage[i % 2][:, :, f0:512],
                                           in_=psct[i % 2][:, :, f0:512],
                                           func=AF.Exp, scale=0.125), 1)

            @block.vector
            def _(v):
                for i, (b, hp, qtc, j, njs, qx) in enumerate(iters):
                    if j >= 4 * qtc:
                        f0 = 128 * (j - 4 * qtc)
                        v.wait_ge(sEXP.h, i + 1)
                        sTRI.inc(v.tensor_tensor(
                            out=pstage[i % 2][:, 0, f0:f0 + 128],
                            in0=pstage[i % 2][:, 0, f0:f0 + 128],
                            in1=trib[:, :], op=OP.mult), 1)
                        sTRI.inc(v.tensor_tensor(
                            out=pstage[i % 2][:, 1, f0:f0 + 128],
                            in0=pstage[i % 2][:, 1, f0:f0 + 128],
                            in1=trib[:, :], op=OP.mult), 1)
                    if j == njs - 1:
                        v.wait_ge(sAV.h, i + 1)
                        pv = 0
                        sRC.inc(v.reciprocal(out=rs_sb[64:65, 0, :],
                                             in_=pav[pv][64:65, :]), 1)
                        sRC.inc(v.reciprocal(out=rs_sb[64:65, 1, :],
                                             in_=pav[pv + 1][64:65, :]), 1)
                        v.wait_ge(sBC.h, 2 * qx + 2)
                        if qx >= 2:
                            v.wait_ge(sSTG.h, 32 * (qx - 1))
                        vf(v, v.tensor_copy(out=rb_sb[0:64, 0, :],
                                            in_=pbc[0:64, 0, :]))
                        vf(v, v.tensor_copy(out=rb_sb[0:64, 1, :],
                                            in_=pbc[0:64, 1, :]))
                        sEPI.inc(v.tensor_tensor(out=stg[2 * (qx % 2)][0:64, :],
                                                 in0=pav[pv][0:64, :],
                                                 in1=rb_sb[0:64, 0, :],
                                                 op=OP.mult), 1)
                        sEPI.inc(v.tensor_tensor(out=stg[2 * (qx % 2) + 1][0:64, :],
                                                 in0=pav[pv + 1][0:64, :],
                                                 in1=rb_sb[0:64, 1, :],
                                                 op=OP.mult), 1)

            @block.sync
            def _(sy):
                for qx in range(32):
                    b, hp, qtc = qx // 16, (qx // 8) % 2, qx % 8
                    qt0 = 4096 * b + 512 * qtc
                    sy.wait_ge(sEPI.h, 2 * qx + 2)
                    hA, hB = 2 * hp, 2 * hp + 1
                    sSTG.inc(sy.dma_start(
                        out=atag_i.ap()[64 * hA:64 * hA + 64, qt0:qt0 + 512],
                        in_=stg[2 * (qx % 2)][0:64, :]), 16)
                    sSTG.inc(sy.dma_start(
                        out=atag_i.ap()[64 * hB:64 * hB + 64, qt0:qt0 + 512],
                        in_=stg[2 * (qx % 2) + 1][0:64, :]), 16)

            @block.gpsimd
            def _(g):
                g.wait_ge(sSTG.h, 32 * 32)
                sAGAc.inc(g.collective_compute(
                    "AllGather", OP.bypass, replica_groups=[list(range(NC))],
                    ins=[atag_i.ap()], outs=[atag_o.ap()]), 1)
                g.wait_ge(sAGAc.h, 1)
                sAGAc.inc(g.collective_compute(
                    "AllReduce", OP.add, replica_groups=[list(range(NC))],
                    ins=[stag_i.ap()], outs=[barr_o.ap()]), 1)

            if t_at is not None:
                @block.sync
                def _(sy):
                    sy.wait_ge(sAGAc.h, 2)
                    sDBG.inc(sy.dma_start(out=t_at.ap(), in_=atag_o.ap()), 16)

            if upto == "D":
                _drain(nc, block, out_d, [sDBG], [sOS])
                return nc, taps

            pD.close()
        qkv_scope.close()

        r2 = ExitStack()
        with r2:
            hT = r2.enter_context(nc.sbuf_tensor("hT", [128, 40 * 512], BF16))
            wpc = [r2.enter_context(nc.sbuf_tensor(f"wpc{i}", [128, 40, 256], BF16))
                   for i in range(2)]
            woc = [r2.enter_context(nc.sbuf_tensor(f"woc{i}", [128, 16, 256], BF16))
                   for i in range(2)]
            atn = r2.enter_context(nc.sbuf_tensor("atn", [128, 16, 512], BF16))
            xre = [r2.enter_context(nc.sbuf_tensor(f"xre{i}", [128, 4, 256], F32))
                   for i in range(2)]
            fsl = [r2.enter_context(nc.sbuf_tensor(f"fsl{i}", [128, 16, 256], BF16))
                   for i in range(2)]
            ost = [r2.enter_context(nc.sbuf_tensor(f"ost{i}", [128, 256], F32))
                   for i in range(4)]
            bout_sb = r2.enter_context(nc.sbuf_tensor("bout_sb", [1, D], F32))
            pEF = ExitStack()
            with pEF:
                ph = [pEF.enter_context(nc.psum_tensor(f"ph{i}", [128, 512], F32))
                      for i in range(4)]
                po = [pEF.enter_context(nc.psum_tensor(f"po{i}", [128, 256], F32))
                      for i in range(2)]

                def _fstores(sy, bx):
                    u2, d22 = bx // 8, bx % 8
                    for tt in range(4):
                        px = 4 * bx + tt
                        sy.wait_ge(sOE.h, px + 1)
                        sOS.inc(sy.dma_start(
                            out=out_d[512 * u2 + 128 * tt:
                                      512 * u2 + 128 * (tt + 1),
                                      256 * d22:256 * (d22 + 1)],
                            in_=ost[px % 4][:, :]), 16)

                @block.sync
                def _(sy):
                    sy.wait_ge(sAV.h, NIT)
                    sy.wait_ge(sEPI.h, 64)
                    sy.wait_ge(sAGWc.h, 2)
                    sy.wait_ge(sAGAc.h, 2)
                    pid = sy.snap(sy.partition_id(), min_val=0, max_val=NC - 1)
                    for u in range(2):
                        for jg in range(20):
                            jgx = 20 * u + jg
                            if jgx >= 2:
                                sy.wait_ge(sFMM.h, jgx - 1)
                            sFS.inc(sy.dma_start(
                                out=fsl[jgx % 2][:, :, :],
                                in_=wfag_o.ap().rearrange(
                                    "(a p) f -> p a f",
                                    p=128)[:, :, 256 * jg:256 * (jg + 1)]), 16)
                        if u == 0:
                            sy.wait_ge(sAGAc.h, 2)
                        else:
                            sy.wait_ge(sOP.h, 32)
                        coff = pid * TOWN + 512 * u
                        sPC.inc(sy.dma_start(
                            out=atn[:, :, :],
                            in_=atag_o.ap().rearrange(
                                "(a p) t -> p a t", p=128)[:, :, ds(coff, 512)]), 16)
                        for d2 in range(8):
                            bx = 8 * u + d2
                            if bx >= 2:
                                sy.wait_ge(sOP.h, 4 * (bx - 1))
                            sPC.inc(sy.dma_start(
                                out=wpc[bx % 2][:, :, :],
                                in_=wpag_o.ap().rearrange(
                                    "(a p) f -> p a f",
                                    p=128)[:, :, 256 * d2:256 * (d2 + 1)]), 16)
                            if bx >= 2:
                                sy.wait_ge(sOE.h, 4 * (bx - 1))
                            sXR.inc(sy.dma_start(
                                out=xre[bx % 2][:, :, :],
                                in_=x_own[512 * u:512 * (u + 1),
                                          256 * d2:256 * (d2 + 1)].rearrange(
                                    "(a p) f -> p a f", p=128)), 16)
                            if bx >= 1:
                                _fstores(sy, bx - 1)
                    _fstores(sy, 15)
                    sy.wait_ge(sOS.h, 16 * 64)
                    if sDBG.n:
                        sy.wait_ge(sDBG.h, sDBG.n)

                @block.gpsimd
                def _(g):
                    g.wait_ge(sSTG.h, 32 * 32)
                    g.wait_ge(sAGAc.h, 2)
                    sWC.inc(g.dma_start(out=bout_sb[:, :], in_=bout_d[:, :]), 16)
                    for u in range(2):
                        for d2 in range(8):
                            bx = 8 * u + d2
                            if bx >= 2:
                                g.wait_ge(sOP.h, 4 * (bx - 1))
                            sWC.inc(g.dma_start(
                                out=woc[bx % 2][:, :, :],
                                in_=wout_d.rearrange(
                                    "(a p) f -> p a f",
                                    p=128)[:, :, 256 * d2:256 * (d2 + 1)]), 16)

                @block.tensor
                def _(t):
                    for u in range(2):
                        for jg in range(20):
                            jgx = 20 * u + jg
                            t.wait_ge(sFS.h, 16 * (jgx + 1))
                            if jgx >= 2:
                                t.wait_ge(sFE.h, 2 * (jgx - 1))
                            if u == 1 and jg == 0:
                                t.wait_ge(sOP.h, 32 * u)
                            for dd in range(16):
                                insA = t.matmul(
                                    ph[2 * (jg % 2)][:, :],
                                    fsl[jgx % 2][:, dd, 0:128],
                                    xnT[:, 1024 * dd + 512 * u:
                                        1024 * dd + 512 * (u + 1)],
                                    start=(dd == 0), stop=(dd == 15))
                                insB = t.matmul(
                                    ph[2 * (jg % 2) + 1][:, :],
                                    fsl[jgx % 2][:, dd, 128:256],
                                    xnT[:, 1024 * dd + 512 * u:
                                        1024 * dd + 512 * (u + 1)],
                                    start=(dd == 0), stop=(dd == 15))
                            sFMM.inc(insB, 1)
                        for d2 in range(8):
                            bx = 8 * u + d2
                            t.wait_ge(sPC.h, 16 * (u + 1) + 16 * (bx + 1))
                            t.wait_ge(sWC.h, 16 + 16 * (bx + 1))
                            if bx == 0:
                                t.wait_ge(sFH.h, 40)
                            elif d2 == 0:
                                t.wait_ge(sFH.h, 80)
                            for tt in range(4):
                                px = 4 * bx + tt
                                if px >= 2:
                                    t.wait_ge(sOE.h, px - 1)
                                first = True
                                for jc in range(40):
                                    ins = t.matmul(
                                        po[px % 2][:, :],
                                        hT[:, 512 * jc + 128 * tt:
                                           512 * jc + 128 * (tt + 1)],
                                        wpc[bx % 2][:, jc, :],
                                        start=first, stop=False)
                                    first = False
                                for dc in range(16):
                                    t.matmul(po[px % 2][:, :],
                                             atn[:, dc, 128 * tt:128 * (tt + 1)],
                                             woc[bx % 2][:, dc, :],
                                             start=False, stop=False)
                                sOP.inc(t.matmul(
                                    po[px % 2][:, :], onesf[0:1, 0:128],
                                    bout_sb[0:1, 256 * d2:256 * (d2 + 1)],
                                    start=False, stop=True), 1)

                @block.scalar
                def _(sc):
                    for u in range(2):
                        for jg in range(20):
                            jgx = 20 * u + jg
                            sc.wait_ge(sFMM.h, jgx + 1)
                            if jgx >= 2:
                                sc.wait_ge(sFH.h, 2 * (jgx - 1))
                            sFE.inc(sc.activation(out=pstage[jg % 2][:, 0, :],
                                                  in_=ph[2 * (jg % 2)][:, :],
                                                  func=AF.Relu), 1)
                            sFE.inc(sc.activation(out=pstage[jg % 2][:, 1, :],
                                                  in_=ph[2 * (jg % 2) + 1][:, :],
                                                  func=AF.Relu), 1)

                @block.vector
                def _(v):
                    for u in range(2):
                        for jg in range(20):
                            jgx = 20 * u + jg
                            v.wait_ge(sFE.h, 2 * (jgx + 1))
                            if u == 1 and jg == 0:
                                v.wait_ge(sOP.h, 32)
                            jc = 2 * jg
                            sFH.inc(v.tensor_tensor(
                                out=hT[:, 512 * jc:512 * (jc + 1)],
                                in0=pstage[jg % 2][:, 0, :],
                                in1=pstage[jg % 2][:, 0, :],
                                op=OP.mult), 1)
                            sFH.inc(v.tensor_tensor(
                                out=hT[:, 512 * (jc + 1):512 * (jc + 2)],
                                in0=pstage[jg % 2][:, 1, :],
                                in1=pstage[jg % 2][:, 1, :], op=OP.mult), 1)
                        for d2 in range(8):
                            bx = 8 * u + d2
                            for tt in range(4):
                                px = 4 * bx + tt
                                v.wait_ge(sOP.h, px + 1)
                                v.wait_ge(sXR.h, 16 * (bx + 1))
                                if px >= 4:
                                    v.wait_ge(sOS.h, 16 * (px - 3))
                                sOE.inc(v.tensor_tensor(out=ost[px % 4][:, :],
                                                        in0=po[px % 2][:, :],
                                                        in1=xre[bx % 2][:, tt, :],
                                                        op=OP.add), 1)


    return nc, taps


def _drain(nc, block, out_d, dbg_ctrs, out_ctrs):
    @block.sync
    def _(sy):
        for c in dbg_ctrs:
            if c.n:
                sy.wait_ge(c.h, c.n)


def _host_inputs(L0_attn_x, L0_attn_Wqkv, L0_attn_bqkv, L0_attn_Wout,
                 L0_attn_bout, L0_ffn_w_fc, L0_ffn_w_proj):
    x = np.ascontiguousarray(np.asarray(L0_attn_x, dtype=np.float32).reshape(T, D))
    Wqkv = np.asarray(L0_attn_Wqkv, dtype=np.float32)
    bqkv = np.asarray(L0_attn_bqkv, dtype=np.float32)
    Wout = np.ascontiguousarray(np.asarray(L0_attn_Wout, dtype=np.float32))
    bout = np.ascontiguousarray(np.asarray(L0_attn_bout, dtype=np.float32)
                                .reshape(1, D))
    wfcT = np.ascontiguousarray(np.asarray(L0_ffn_w_fc, dtype=np.float32).T)
    wprT = np.ascontiguousarray(np.asarray(L0_ffn_w_proj, dtype=np.float32).T)
    maps = []
    for c in range(NC):
        qs = slice(QC * c, QC * (c + 1))
        ks = slice(HQ * HD + HD * c, HQ * HD + HD * (c + 1))
        vs = slice((HQ + HKV) * HD + HD * c, (HQ + HKV) * HD + HD * (c + 1))
        bk = bqkv[ks]
        bv = bqkv[vs]
        bkv = np.ascontiguousarray(np.concatenate([bk, bv]).reshape(128, 1))
        bkd = np.ascontiguousarray(np.concatenate([np.zeros(64, np.float32), bk])
                                   .reshape(128, 1))
        maps.append({
            "x_own": np.ascontiguousarray(x[TOWN * c:TOWN * (c + 1)]),
            "wq": np.ascontiguousarray(Wqkv[:, qs]),
            "wk": np.ascontiguousarray(Wqkv[:, ks]),
            "wv": np.ascontiguousarray(Wqkv[:, vs]),
            "bq": np.ascontiguousarray(bqkv[qs].reshape(QC, 1)),
            "bkv": bkv,
            "bkd": bkd,
            "wout": Wout,
            "bout": bout,
            "wfcT_sh": np.ascontiguousarray(wfcT[(D // NC) * c:(D // NC) * (c + 1)]),
            "wprT_sh": np.ascontiguousarray(wprT[(FF // NC) * c:(FF // NC) * (c + 1)]),
        })
    return maps


def kernel(L0_attn_x, L0_attn_Wqkv, L0_attn_bqkv, L0_attn_Wout, L0_attn_bout,
           L0_ffn_w_fc, L0_ffn_w_proj):
    maps = _host_inputs(L0_attn_x, L0_attn_Wqkv, L0_attn_bqkv, L0_attn_Wout,
                        L0_attn_bout, L0_ffn_w_fc, L0_ffn_w_proj)
    nc, _ = build_program()
    res = run_bass_kernel_spmd(nc, maps, core_ids=list(range(NC)))
    out = np.concatenate([res.results[c]["out"] for c in range(NC)], axis=0)
    return np.ascontiguousarray(out.reshape(B, S, D))
